# version 11
# baseline (speedup 1.0000x reference)
from contextlib import ExitStack
import numpy as np
import ml_dtypes
import concourse.bass as bass
import concourse.mybir as mybir
from concourse.bass_utils import run_bass_kernel_spmd

F32 = mybir.dt.float32
BF16 = mybir.dt.bfloat16
I32 = mybir.dt.int32
AF = mybir.ActivationFunctionType
OP = mybir.AluOpType
AX = mybir.AxisListType

D = 2048
T = 2048
HID = 8192
NB = D // 128
KI = HID // 128
W = 512
NW = T // W
TQ = 4
WQ = T // TQ
EPS = 1e-5
C32 = float(np.float32(1.5 * 2 ** 23))
C52 = float(np.float32(1.5 * 2 ** 52))


class Sem:
    def __init__(self, nc, st, name):
        self.h = st.enter_context(nc.semaphore(name))
        self.n = 0

    def inc(self, instr, k=1):
        instr.then_inc(self.h, k)
        self.n += k
        return instr

    def dma(self, instr):
        return self.inc(instr, 16)


def build(dbg=False, dbgwin=None):
    if dbgwin is None:
        dbgwin = dbg
    nc = bass.Bass("TRN2", target_bir_lowering=False, detect_race_conditions=False)

    xT = nc.dram_tensor("xT", [D, T], F32, kind="ExternalInput").ap()
    Wr = nc.dram_tensor("Wr", [D, D], F32, kind="ExternalInput").ap()
    Wf = nc.dram_tensor("Wf", [D, D], F32, kind="ExternalInput").ap()
    Wc = nc.dram_tensor("Wc", [D, D], F32, kind="ExternalInput").ap()
    Wo = nc.dram_tensor("Wo", [D, D], F32, kind="ExternalInput").ap()
    lns = nc.dram_tensor("lns", [128, NB], F32, kind="ExternalInput").ap()
    wgT = nc.dram_tensor("wgT", [D, HID], F32, kind="ExternalInput").ap()
    wuT = nc.dram_tensor("wuT", [D, HID], F32, kind="ExternalInput").ap()
    wdT = nc.dram_tensor("wdT", [HID, D], F32, kind="ExternalInput").ap()
    outT = nc.dram_tensor("outT", [D, T], F32, kind="ExternalOutput").ap()

    KIND = "ExternalOutput" if dbg else "Internal"
    if dbg:
        f_dbg = nc.dram_tensor("f_dbg", [D, T], F32, kind="ExternalOutput").ap()
        c_dbg = nc.dram_tensor("c_dbg", [D, T], F32, kind="ExternalOutput").ap()
        h_dbg = nc.dram_tensor("h_dbg", [D, T], F32, kind="ExternalOutput").ap()
        r_dbg = nc.dram_tensor("r_dbg", [D, T], F32, kind="ExternalOutput").ap()
        g_dbg = nc.dram_tensor("g_dbg", [D, T], BF16, kind="ExternalOutput").ap()
    xq_sp = nc.dram_tensor("xq_sp", [D, T], BF16, kind=KIND).ap()
    gu_sp = nc.dram_tensor("gu_sp", [HID, T], F32, kind=KIND).ap()
    wdq_sp = nc.dram_tensor("wdq_sp", [HID, D], BF16, kind=KIND).ap()
    oacc_sp = nc.dram_tensor("oacc_sp", [D, T], F32, kind=KIND).ap()
    rows_sp = nc.dram_tensor("rows_sp", [8, T], F32, kind=KIND).ap()

    st = ExitStack()
    sbt = lambda n_, sh, dt: st.enter_context(nc.sbuf_tensor(n_, sh, dt))
    with st:
        actA = sbt("actA", [128, NB * T], BF16)
        tmpC1 = sbt("tmpC1", [128, 2 * T], F32)
        rows = sbt("rows", [1, 2 * T], F32)
        bc0 = sbt("bc0", [128, T], F32)
        bc1 = sbt("bc1", [128, T], F32)
        cols = sbt("cols", [128, 64], F32)
        d_i32 = sbt("d_i32", [128, 1], I32)
        scaf = sbt("scaf", [128, 32], F32)
        psum = st.enter_context(nc.psum_tensor("ps", [128, 4096], F32))
        BANK = lambda b: psum[:, b * 512:(b + 1) * 512]
        PHALF = lambda h: psum[:, h * 2048:h * 2048 + 2048]

        CC = cols[:, 0:1]
        LNSC = lambda b: cols[:, 1 + b:2 + b]
        CAR = lambda b: cols[:, 17 + b:18 + b]
        INV127 = cols[:, 33:34]
        R0 = rows[:, 0:T]
        R1 = rows[:, T:2 * T]

        s_ld = Sem(nc, st, "s_ld")
        s_wld = Sem(nc, st, "s_wld")
        s_g = Sem(nc, st, "s_g")
        s_v = Sem(nc, st, "s_v")
        s_s = Sem(nc, st, "s_s")
        s_pe = Sem(nc, st, "s_pe")
        s_cast = Sem(nc, st, "s_cast")
        s_row = Sem(nc, st, "s_row")
        s_sp = Sem(nc, st, "s_sp")
        s_fin = Sem(nc, st, "s_fin")

        V, S, G, PE, SY = nc.vector, nc.scalar, nc.gpsimd, nc.tensor, nc.sync
        SQ = nc.scalar

        def part_max_tree(a, b, c, psum_src=None):
            if psum_src is not None:
                s_v.inc(V.tensor_copy(a, psum_src))
            tri = [a, b, c]
            cur = 0
            for n in (64, 32, 16, 8, 4, 2, 1):
                aa, bb, cc = tri[cur % 3], tri[(cur + 1) % 3], tri[(cur + 2) % 3]
                s_v.inc(V.nop()) if False else None
                SY.wait_ge(s_v.h, s_v.n)
                s_row.dma(SY.dma_start(bb[0:n, :], aa[n:2 * n, :]))
                V.wait_ge(s_row.h, s_row.n)
                s_v.inc(V.scalar_tensor_tensor(cc[0:n, :], bb[0:n, :], 1.0, aa[0:n, :],
                                               OP.mult, OP.max))
                cur += 2
            return tri[cur % 3]

        s_v.inc(V.memset(scaf[:, 0:1], 0.7))
        V.wait_ge(s_v.h, s_v.n)
        s_v.inc(V.tensor_scalar(d_i32[:, :], scaf[:, 0:1], 0.0, None, OP.add))
        G.wait_ge(s_v.h, s_v.n)
        s_cast.dma(G.dma_start(scaf[:, 1:2], d_i32[:, :]))
        V.wait_ge(s_cast.h, s_cast.n)
        V.tensor_scalar(scaf[:, 2:3], scaf[:, 1:2], C32, None, OP.mult)
        V.tensor_scalar(scaf[:, 3:4], scaf[:, 1:2], -C52, C52, OP.mult, OP.add)
        V.tensor_add(CC, scaf[:, 2:3], scaf[:, 3:4])
        V.tensor_scalar(cols[:, 34:35], scaf[:, 1:2], -3e-7, 1.0, OP.mult, OP.add)
        s_v.inc(V.memset(scaf[:, 14:15], 1.0))
        ONES = scaf[:, 14:15]
        s_v.inc(V.memset(scaf[:, 15:16], 127.0))
        V.wait_ge(s_v.h, s_v.n)
        s_v.inc(V.reciprocal(INV127, scaf[:, 15:16]))
        V_CONST = s_v.n
        s_ld.dma(SY.dma_start(cols[:, 1:17], lns))
        LNS_LD = s_ld.n

        stP12 = ExitStack()
        gld = stP12.enter_context(nc.sbuf_tensor("gld", [128, 2 * 2048], F32))
        junk2 = stP12.enter_context(nc.sbuf_tensor("junk2", [128, 2048], F32))
        tmpC2 = stP12.enter_context(nc.sbuf_tensor("tmpC2", [128, 2 * T], F32))
        xstg = stP12.enter_context(nc.sbuf_tensor("xstg", [128, T], BF16))
        with stP12:
            p1chunks = []
            for mat, nrow, wid, slot in ((wgT, NB, HID, 4), (wuT, NB, HID, 5), (wdT, KI, D, 6)):
                for rchunk in range(nrow):
                    for c0 in range(0, wid, 2048):
                        p1chunks.append((mat, rchunk, c0, 2048, slot))
            accslot = {4: (4, 20), 5: (5, 21), 6: (6, 22)}
            nacc = {4: 0, 5: 0, 6: 0}
            gdone = []
            sdone = []
            for ci, (mat, rc, c0, wid, slot) in enumerate(p1chunks):
                buf = gld[:, (ci % 2) * 2048:(ci % 2) * 2048 + wid]
                if ci >= 2:
                    SQ.wait_ge(s_g.h, gdone[ci - 2])
                s_wld.dma(SQ.dma_start(buf, mat[rc * 128:(rc + 1) * 128, c0:c0 + wid]))
                S.wait_ge(s_wld.h, 16 * (ci + 1))
                par = scaf[:, 8 + ci % 2:9 + ci % 2]
                if ci >= 2:
                    S.wait_ge(s_g.h, gdone[ci - 1])
                s_s.inc(S.activation(junk2[:, 0:wid], buf, AF.Abs, accum_out=par))
                sdone.append(s_s.n)
                G.wait_ge(s_s.h, sdone[ci])
                a, b = accslot[slot]
                k = nacc[slot]
                if k == 0:
                    s_g.inc(G.tensor_copy(scaf[:, a:a + 1], par))
                else:
                    psrc = scaf[:, (a if k % 2 == 1 else b):(a if k % 2 == 1 else b) + 1]
                    pdst = scaf[:, (b if k % 2 == 1 else a):(b if k % 2 == 1 else a) + 1]
                    s_g.inc(G.tensor_tensor(pdst, par, psrc, OP.add))
                nacc[slot] += 1
                gdone.append(s_g.n)
            for slot in (4, 5, 6):
                a, b = accslot[slot]
                fin = a if (nacc[slot] % 2 == 1) else b
                if fin != slot:
                    s_g.inc(G.tensor_copy(scaf[:, slot:slot + 1], scaf[:, fin:fin + 1]))
            G_PARTIALS = s_g.n
            SCg, SCu, SCd = scaf[:, 16:17], scaf[:, 17:18], scaf[:, 18:19]
            INVg, INVu, INVd = scaf[:, 10:11], scaf[:, 11:12], scaf[:, 12:13]

            xblk = lambda b: xT[b * 128:(b + 1) * 128, :]
            XA = lambda b: tmpC1[:, (b % 2) * T:(b % 2) * T + T]
            XSQ = lambda b: tmpC2[:, (b % 2) * T:(b % 2) * T + T]
            pa0 = s_ld.n
            pe_blk = []
            for b in range(NB):
                if b >= 2:
                    SY.wait_ge(s_pe.h, pe_blk[b - 2])
                s_ld.dma(SY.dma_start(XA(b), xblk(b)))
                PE.wait_ge(s_ld.h, pa0 + 16 * (b + 1))
                if b == 0:
                    PE.wait_ge(s_v.h, V_CONST)
                for w in range(NW):
                    mm = PE.matmul(psum[0:1, w * W:(w + 1) * W], ONES, XA(b)[:, w * W:(w + 1) * W],
                                   start=(b == 0), stop=(b == NB - 1))
                s_pe.inc(mm)
                pe_blk.append(s_pe.n)
            ra = tmpC2[0:1, 0:T]
            rb = tmpC2[0:1, T:2 * T]
            V.wait_ge(s_pe.h, s_pe.n)
            V.tensor_scalar(R0, psum[0:1, 0:2048], 1.0 / D, None, OP.mult)
            s_v.inc(V.tensor_scalar(R1, R0, -1.0, None, OP.mult))
            SY.wait_ge(s_v.h, s_v.n)
            s_row.dma(SY.dma_start(rows_sp[1:2, :], R1))
            SY.wait_ge(s_row.h, s_row.n)
            s_row.dma(SY.dma_start(bc0[:, :], rows_sp[1:2, :].broadcast_to([128, T])))
            ROW_NMU = s_row.n
            pa1 = s_ld.n
            pe_blk2 = []
            v_blk0 = []
            ss2 = []
            for b in range(NB):
                if b >= 2:
                    SY.wait_ge(s_pe.h, pe_blk2[b - 2])
                s_ld.dma(SY.dma_start(XA(b), xblk(b)))
                V.wait_ge(s_ld.h, pa1 + 16 * (b + 1))
                if b == 0:
                    V.wait_ge(s_row.h, ROW_NMU)
                if b >= 2:
                    V.wait_ge(s_s.h, ss2[b - 2])
                s_v.inc(V.tensor_add(XSQ(b), XA(b), bc0[:, :]))
                v_blk0.append(s_v.n)
                S.wait_ge(s_v.h, v_blk0[b])
                if b >= 2:
                    S.wait_ge(s_pe.h, pe_blk2[b - 2])
                s_s.inc(S.activation(XA(b), XSQ(b), AF.Square))
                ss2.append(s_s.n)
                PE.wait_ge(s_s.h, s_s.n)
                for w in range(NW):
                    mm = PE.matmul(psum[0:1, 2048 + w * W:2048 + (w + 1) * W], ONES,
                                   XA(b)[:, w * W:(w + 1) * W], start=(b == 0), stop=(b == NB - 1))
                s_pe.inc(mm)
                pe_blk2.append(s_pe.n)
            V.wait_ge(s_pe.h, s_pe.n)
            s_v.inc(V.tensor_scalar(rb, psum[0:1, 2048:4096], 1.0 / D, EPS, OP.mult, OP.add))
            V_STATS = s_v.n
            S.wait_ge(s_v.h, s_v.n)
            s_s.inc(S.activation(ra, rb, AF.Sqrt))
            V.wait_ge(s_s.h, s_s.n)
            V.reciprocal(R0, ra)
            V.tensor_mul(ra, R0, R0)
            V.tensor_mul(ra, ra, rb)
            V.tensor_scalar(ra, ra, -0.5, 1.5, OP.mult, OP.add)
            V.tensor_mul(rb, R0, ra)
            V.tensor_mul(ra, R1, rb)
            V.tensor_copy(R1, rb)
            s_v.inc(V.tensor_copy(R0, ra))
            SY.wait_ge(s_v.h, s_v.n)
            s_row.dma(SY.dma_start(rows_sp[0:1, :], R1))
            s_row.dma(SY.dma_start(rows_sp[1:2, :], R0))
            SY.wait_ge(s_row.h, s_row.n)
            s_row.dma(SY.dma_start(bc0[:, :], rows_sp[0:1, :].broadcast_to([128, T])))
            s_row.dma(SY.dma_start(bc1[:, :], rows_sp[1:2, :].broadcast_to([128, T])))
            ROW_RS = s_row.n
            xnT = lambda b: actA[:, b * T:(b + 1) * T]
            UB1 = lambda b: tmpC2[:, (b % 2) * T:(b % 2) * T + T]
            s_v.inc(V.memset(PHALF(1), 0.0))
            V.wait_ge(s_v.h, s_v.n)
            pb0 = s_ld.n
            v_blk = []
            for b in range(NB):
                if b >= 2:
                    SY.wait_ge(s_v.h, v_blk[b - 2])
                s_ld.dma(SY.dma_start(XA(b), xblk(b)))
                V.wait_ge(s_ld.h, pb0 + 16 * (b + 1))
                if b == 0:
                    V.wait_ge(s_row.h, ROW_RS)
                u = UB1(b)
                V.tensor_mul(u, XA(b), bc0[:, :])
                V.scalar_tensor_tensor(xnT(b), u, 1.0, bc1[:, :], OP.mult, OP.add)
                V.scalar_tensor_tensor(XA(b), u, 1.0, bc1[:, :], OP.mult, OP.add)
                V.scalar_tensor_tensor(u, XA(b), -1.0, XA(b), OP.mult, OP.max)
                s_v.inc(V.scalar_tensor_tensor(PHALF(b % 2), u, 1.0, PHALF(1 - b % 2),
                                               OP.mult, OP.max))
                v_blk.append(s_v.n)
            TRA = tmpC2[:, 0:T]
            TRB = tmpC2[:, T:2 * T]
            TRC = bc0[:, :]
            TOPROW = part_max_tree(TRA, TRB, TRC, psum_src=PHALF(1))
            V_PSUMFREE = s_v.n
            V.tensor_scalar(R0, TOPROW[0:1, :], INV127[0:1, :], EPS, OP.mult, OP.add)
            s_v.inc(V.reciprocal(R1, R0))
            SY.wait_ge(s_v.h, s_v.n)
            s_row.dma(SY.dma_start(rows_sp[2:3, :], R0))
            SY.wait_ge(s_row.h, s_row.n)
            s_row.dma(SY.dma_start(tmpC2[0:1, 0:T], rows_sp[0:1, :]))
            s_row.dma(SY.dma_start(tmpC2[0:1, T:2 * T], rows_sp[1:2, :]))
            V.wait_ge(s_row.h, s_row.n)
            V.tensor_mul(R0, ra, R1)
            V.tensor_mul(ra, rb, R1)
            s_v.inc(V.tensor_copy(R1, ra))
            SY.wait_ge(s_v.h, s_v.n)
            s_row.dma(SY.dma_start(rows_sp[4:5, :], R0))
            s_row.dma(SY.dma_start(rows_sp[5:6, :], R1))
            SY.wait_ge(s_row.h, s_row.n)
            s_row.dma(SY.dma_start(bc0[:, :], rows_sp[4:5, :].broadcast_to([128, T])))
            s_row.dma(SY.dma_start(bc1[:, :], rows_sp[5:6, :].broadcast_to([128, T])))
            ROW_RQ = s_row.n
            pc0 = s_ld.n
            v_blk2 = []
            sp_blk2 = []
            for b in range(NB):
                if b >= 2:
                    SY.wait_ge(s_v.h, v_blk2[b - 2])
                s_ld.dma(SY.dma_start(XA(b), xblk(b)))
                V.wait_ge(s_ld.h, pc0 + 16 * (b + 1))
                if b == 0:
                    V.wait_ge(s_row.h, ROW_RQ)
                    V.wait_ge(s_v.h, V_CONST)
                if b >= 1:
                    V.wait_ge(s_sp.h, sp_blk2[b - 1])
                u = UB1(b)
                V.tensor_mul(u, XA(b), bc0[:, :])
                V.scalar_tensor_tensor(XA(b), u, 1.0, bc1[:, :], OP.mult, OP.add)
                s_v.inc(V.tensor_scalar(xstg[:, :], XA(b), CC, CC, OP.add, OP.subtract))
                v_blk2.append(s_v.n)
                SY.wait_ge(s_v.h, v_blk2[b])
                s_sp.dma(SY.dma_start(xq_sp[b * 128:(b + 1) * 128, :], xstg[:, :]))
                sp_blk2.append(s_sp.n)
            XQ_SPILLED = s_sp.n
        stP3 = ExitStack()
        actB = stP3.enter_context(nc.sbuf_tensor("actB", [128, NB * T], BF16))
        wbuf = stP3.enter_context(nc.sbuf_tensor("wbuf", [128, 6 * NB * 128], BF16))
        heven = stP3.enter_context(nc.sbuf_tensor("heven", [128, T // 2], F32))
        pwin = stP3.enter_context(nc.sbuf_tensor("pwin", [128, W], BF16))
        with stP3:
            WSL = lambda s: wbuf[:, s * NB * 128:(s + 1) * NB * 128].rearrange("p (k c) -> p k c", c=128)
            castsrc = lambda mat, ob: mat.rearrange("(k p) (o c) -> p k o c", p=128, c=128)[:, :, ob, :]
            Gblk = lambda b: actB[:, b * T:(b + 1) * T]
            fbF = tmpC1[:, 0:T]
            cbF = tmpC1[:, T:2 * T]
            _pyr = {}
            off0 = 0
            for k in range(1, 12):
                n = T >> k
                _pyr[k] = n
            pyr_f = {1: bc0[:, 0:1024], 2: bc1[:, 0:512]}
            pyr_c = {1: bc0[:, 1024:2048], 2: bc1[:, 512:1024]}
            off = 1024
            for k in range(3, 12):
                n = T >> k
                pyr_f[k] = bc1[:, off:off + n]; off += n
                pyr_c[k] = bc1[:, off:off + n]; off += n
            lev_f = {0: fbF}
            lev_c = {0: cbF}
            lev_f.update(pyr_f)
            lev_c.update(pyr_c)
            PS_T3 = lambda n: psum[:, 0:n]
            PS_T4 = lambda n: psum[:, 2048:2048 + n]

            def vbar(ins):
                s_v.inc(ins)
                V.wait_ge(s_v.h, s_v.n)
                return ins
            V.wait_ge(s_ld.h, LNS_LD)
            G.wait_ge(s_wld.h, s_wld.n)
            G.wait_ge(s_sp.h, XQ_SPILLED)
            G.wait_ge(s_v.h, s_v.n)
            G.wait_ge(s_row.h, s_row.n)
            cast_ob = []
            for ob in range(2):
                for mi, mat in enumerate((Wf, Wc, Wr)):
                    s_cast.dma(G.dma_start(WSL(3 * (ob % 2) + mi), castsrc(mat, ob)))
                cast_ob.append(s_cast.n)
            prev_s = s_s.n
            prev_v = s_v.n
            rowacc_v = []
            for ob in range(NB):
                sl = 3 * (ob % 2)
                PE.wait_ge(s_cast.h, cast_ob[ob])
                PE.wait_ge(s_s.h, prev_s)
                PE.wait_ge(s_v.h, prev_v)
                if ob == 0:
                    PE.wait_ge(s_v.h, V_PSUMFREE)
                for tw in range(NW):
                    for k in range(NB):
                        PE.matmul(BANK(tw), WSL(sl + 0)[:, k, :], xnT(k)[:, tw * W:(tw + 1) * W],
                                  start=(k == 0), stop=(k == NB - 1))
                for tw in range(NW):
                    for k in range(NB):
                        mm = PE.matmul(BANK(4 + tw), WSL(sl + 1)[:, k, :], xnT(k)[:, tw * W:(tw + 1) * W],
                                       start=(k == 0), stop=(k == NB - 1))
                s_pe.inc(mm)
                PE_FC = s_pe.n
                S.wait_ge(s_pe.h, PE_FC)
                for tw in range(NW):
                    S.activation(fbF[:, tw * W:(tw + 1) * W], BANK(tw), AF.Sigmoid)
                for tw in range(NW):
                    ins_ = S.activation(cbF[:, tw * W:(tw + 1) * W], BANK(4 + tw), AF.Silu)
                s_s.inc(ins_)
                S_FC = s_s.n
                if dbgwin:
                    SY.wait_ge(s_s.h, S_FC)
                    s_sp.dma(SY.dma_start(f_dbg[ob * 128:(ob + 1) * 128, :], fbF))
                    s_sp.dma(SY.dma_start(c_dbg[ob * 128:(ob + 1) * 128, :], cbF))
                    SY_DBG_FC = s_sp.n
                V.wait_ge(s_s.h, S_FC)
                for k in range(0, 11):
                    n = T >> (k + 1)
                    sf, sc = lev_f[k], lev_c[k]
                    df, dc = lev_f[k + 1], lev_c[k + 1]
                    fa = sf[:, 0:2 * n:2]
                    fb = sf[:, 1:2 * n:2]
                    ca = sc[:, 0:2 * n:2]
                    cb = sc[:, 1:2 * n:2]
                    vbar(V.tensor_mul(df, fa, fb))
                    vbar(V.tensor_sub(PS_T3(n), ca, cb))
                    vbar(V.tensor_mul(heven[:, 0:n], PS_T3(n), fb))
                    vbar(V.tensor_add(dc, heven[:, 0:n], cb))
                for k in range(10, 0, -1):
                    n = T >> k
                    half = n // 2
                    of = lev_f[k + 1]
                    oc = lev_c[k + 1]
                    ef_src = lev_f[k]
                    ec_src = lev_c[k]
                    m = half - 1
                    if m > 0:
                        xf = ef_src[:, 2:n:2]
                        xc = ec_src[:, 2:n:2]
                        vbar(V.tensor_mul(PS_T4(m), of[:, 0:m], xf))
                        vbar(V.tensor_sub(PS_T3(m), oc[:, 0:m], xc))
                        vbar(V.tensor_mul(heven[:, 0:m], PS_T3(m), xf))
                        vbar(V.tensor_add(psum[:, 1024:1024 + m], heven[:, 0:m], xc))
                        vbar(V.tensor_copy(ec_src[:, 2:n:2], psum[:, 1024:1024 + m]))
                        vbar(V.tensor_copy(ef_src[:, 2:n:2], PS_T4(m)))
                    vbar(V.tensor_copy(ec_src[:, 1:n:2], oc[:, 0:half]))
                    vbar(V.tensor_copy(ef_src[:, 1:n:2], of[:, 0:half]))
                m = T // 2 - 1
                xf0 = fbF[:, 2:T:2]
                xc0 = cbF[:, 2:T:2]
                vbar(V.tensor_sub(PS_T3(m), pyr_c[1][:, 0:m], xc0))
                vbar(V.tensor_mul(psum[:, 1024:1024 + m], PS_T3(m), xf0))
                vbar(V.tensor_add(heven[:, 1:T // 2], psum[:, 1024:1024 + m], xc0))
                vbar(V.tensor_copy(heven[:, 0:1], cbF[:, 0:1]))
                if dbgwin:
                    SY.wait_ge(s_v.h, s_v.n + 0) if False else None
                if dbgwin:
                    s_v.inc(V.nop()) if False else None
                    _hm = s_v.n
                    s_v.inc(V.tensor_copy(pwin[:, 0:1], heven[:, 0:1]))
                    SY.wait_ge(s_v.h, s_v.n)
                    s_sp.dma(SY.dma_start(h_dbg[ob * 128:(ob + 1) * 128, 0:T // 2], heven[:, :]))
                    s_sp.dma(SY.dma_start(h_dbg[ob * 128:(ob + 1) * 128, T // 2:T], pyr_c[1]))
                    SY.wait_ge(s_sp.h, s_sp.n)
                vbar(V.tensor_mul(fbF[:, 0:T:2], heven[:, :], heven[:, :]))
                vbar(V.tensor_mul(fbF[:, 1:T:2], pyr_c[1], pyr_c[1]))
                V_H2 = s_v.n
                PE.wait_ge(s_v.h, V_H2)
                for tw in range(NW):
                    mm = PE.matmul(psum[0:1, tw * W:(tw + 1) * W], ONES, fbF[:, tw * W:(tw + 1) * W],
                                   start=True, stop=True)
                s_pe.inc(mm)
                V.wait_ge(s_pe.h, s_pe.n)
                if ob == 0:
                    s_v.inc(V.tensor_copy(R0, psum[0:1, 0:2048]))
                else:
                    dst = R0 if ob % 2 == 0 else R1
                    srcr = R1 if ob % 2 == 0 else R0
                    s_v.inc(V.tensor_add(dst, psum[0:1, 0:2048], srcr))
                rowacc_v.append(s_v.n)
                PE.wait_ge(s_v.h, rowacc_v[-1])
                for tw in range(NW):
                    for k in range(NB):
                        mm = PE.matmul(BANK(tw), WSL(sl + 2)[:, k, :], xnT(k)[:, tw * W:(tw + 1) * W],
                                       start=(k == 0), stop=(k == NB - 1))
                s_pe.inc(mm)
                PE_R = s_pe.n
                if ob + 2 < NB:
                    G.wait_ge(s_pe.h, PE_R)
                    for mi, mat in enumerate((Wf, Wc, Wr)):
                        s_cast.dma(G.dma_start(WSL(3 * (ob % 2) + mi), castsrc(mat, ob + 2)))
                    cast_ob.append(s_cast.n)
                else:
                    cast_ob.append(s_cast.n)
                S.wait_ge(s_pe.h, PE_R)
                for tw in range(NW):
                    ins_ = S.activation(BANK(4 + tw), BANK(tw), AF.Silu)
                s_s.inc(ins_)
                prev_s = s_s.n
                V.wait_ge(s_s.h, prev_s)
                for tw in range(NW):
                    rw = BANK(4 + tw)
                    V.tensor_mul(pwin[:, 0:W:2], rw[:, 0:W:2], heven[:, tw * 256:(tw + 1) * 256])
                    vbar(V.tensor_mul(pwin[:, 1:W:2], rw[:, 1:W:2], pyr_c[1][:, tw * 256:(tw + 1) * 256]))
                    s_v.inc(V.tensor_scalar(Gblk(ob)[:, tw * W:(tw + 1) * W], pwin[:, :],
                                            LNSC(ob), None, OP.mult))
                prev_v = s_v.n
                if dbgwin:
                    SY.wait_ge(s_v.h, prev_v)
                    s_sp.dma(SY.dma_start(g_dbg[ob * 128:(ob + 1) * 128, :], Gblk(ob)))
                    s_sp.dma(SY.dma_start(r_dbg[ob * 128:(ob + 1) * 128, :], fbF))
                    SY.wait_ge(s_sp.h, s_sp.n)
            ra3 = tmpC1[0:1, 0:T]
            rb3 = tmpC1[0:1, T:2 * T]
            MSROW = R1
            s_v.inc(V.tensor_scalar(rb3, MSROW, 1.0 / D, EPS, OP.mult, OP.add))
            V_MSROW = s_v.n
            S.wait_ge(s_v.h, V_MSROW)
            s_s.inc(S.activation(ra3, rb3, AF.Sqrt))
            V.wait_ge(s_s.h, s_s.n)
            V.reciprocal(R0, ra3)
            V.tensor_mul(ra3, R0, R0)
            V.tensor_mul(ra3, ra3, rb3)
            V.tensor_scalar(ra3, ra3, -0.5, 1.5, OP.mult, OP.add)
            s_v.inc(V.tensor_mul(R1, R0, ra3))
            SY.wait_ge(s_v.h, s_v.n)
            s_row.dma(SY.dma_start(rows_sp[6:7, :], R1))
            SY.wait_ge(s_row.h, s_row.n)
            SY.wait_ge(s_v.h, s_v.n)
            s_row.dma(SY.dma_start(bc0[:, :], rows_sp[6:7, :].broadcast_to([128, T])))
            ROW_RSH = s_row.n
            OUTB = tmpC1[:, T:2 * T]
            V_GP = s_v.n
            oy_v = []
            oy_sp = []
            for ob in range(NB):
                G.wait_ge(s_pe.h, s_pe.n)
                s_cast.dma(G.dma_start(WSL(ob % 2), castsrc(Wo, ob)))
                PE.wait_ge(s_cast.h, s_cast.n)
                if ob == 0:
                    PE.wait_ge(s_v.h, V_GP)
                if ob >= 1:
                    PE.wait_ge(s_v.h, oy_v[ob - 1])
                for tw in range(NW):
                    if tw >= 1:
                        PE.wait_ge(s_v.h, s_v.n)
                    for k in range(NB):
                        mm = PE.matmul(BANK(7), WSL(ob % 2)[:, k, :],
                                       Gblk(k)[:, tw * W:(tw + 1) * W],
                                       start=(k == 0), stop=(k == NB - 1))
                    s_pe.inc(mm)
                    V.wait_ge(s_pe.h, s_pe.n)
                    if ob == 0 and tw == 0:
                        V.wait_ge(s_row.h, ROW_RSH)
                    if ob >= 1 and tw == 0:
                        V.wait_ge(s_sp.h, oy_sp[ob - 1])
                    s_v.inc(V.tensor_mul(OUTB[:, tw * W:(tw + 1) * W], BANK(7),
                                         bc0[:, tw * W:(tw + 1) * W]))
                oy_v.append(s_v.n)
                SY.wait_ge(s_v.h, oy_v[ob])
                s_sp.dma(SY.dma_start(oacc_sp[ob * 128:(ob + 1) * 128, :], OUTB))
                oy_sp.append(s_sp.n)
            OACC_SP = s_sp.n
            PE.wait_ge(s_g.h, G_PARTIALS)
            for i, slot in enumerate((4, 5, 6)):
                mm = PE.matmul(psum[0:1, 1536 + i:1537 + i], scaf[:, slot:slot + 1], ONES,
                               start=True, stop=True)
            s_pe.inc(mm)
            V.wait_ge(s_pe.h, s_pe.n)
            V.tensor_scalar(scaf[0:1, 20:23], psum[0:1, 1536:1539],
                            1.0 / (D * HID), EPS, OP.mult, OP.add)
            s_v.inc(V.tensor_scalar(scaf[0:1, 16:19], scaf[0:1, 20:23],
                                    cols[0:1, 34:35], None, OP.mult))
            SY.wait_ge(s_v.h, s_v.n)
            s_row.dma(SY.dma_start(rows_sp[7:8, 0:3], scaf[0:1, 16:19]))
            s_row.dma(SY.dma_start(rows_sp[7:8, 3:6], scaf[0:1, 0:3]))
            s_row.dma(SY.dma_start(rows_sp[7:8, 6:7], cols[0:1, 0:1]))
            SY.wait_ge(s_row.h, s_row.n)
            s_row.dma(SY.dma_start(scaf[:, 16:19], rows_sp[7:8, 0:3].broadcast_to([128, 3])))
            ROW_SCALES = s_row.n
            V.wait_ge(s_row.h, ROW_SCALES)
            s_v.inc(V.reciprocal(scaf[:, 10:13], scaf[:, 16:19]))
            V_INVS = s_v.n
            P3_PE = s_pe.n
            P3_V = s_v.n
        stP4 = ExitStack()
        sgb = stP4.enter_context(nc.sbuf_tensor("sgb", [128, T], F32))
        wldF = stP4.enter_context(nc.sbuf_tensor("wldF", [128, 2 * D], F32))
        wqB = stP4.enter_context(nc.sbuf_tensor("wqB", [128, 2 * D], BF16))
        mxb = stP4.enter_context(nc.sbuf_tensor("mxb", [128, 4 * T], F32))
        tmpC3 = stP4.enter_context(nc.sbuf_tensor("tmpC3", [128, 2 * D], F32))
        with stP4:
            MAXP = lambda i: mxb[:, (i % 2) * T:(i % 2) * T + T]
            MINN = lambda i: mxb[:, (2 + i % 2) * T:(2 + i % 2) * T + T]
            UQ1 = tmpC3[:, 0:D]
            UQ2 = tmpC3[:, D:2 * D]
            SY.wait_ge(s_pe.h, P3_PE)
            SY.wait_ge(s_sp.h, s_sp.n)
            xq0 = s_ld.n
            for b in range(NB):
                s_ld.dma(SY.dma_start(xnT(b), xq_sp[b * 128:(b + 1) * 128, :]))
            XQ_LD = s_ld.n
            SQ.wait_ge(s_pe.h, P3_PE)
            wd0 = s_wld.n
            wdsp = []
            for ch in range(KI):
                buf = wldF[:, (ch % 2) * D:(ch % 2) * D + D]
                qb = wqB[:, (ch % 2) * D:(ch % 2) * D + D]
                if ch >= 2:
                    SQ.wait_ge(s_sp.h, wdsp[ch - 2])
                s_wld.dma(SQ.dma_start(buf, wdT[ch * 128:(ch + 1) * 128, :]))
                V.wait_ge(s_wld.h, wd0 + 16 * (ch + 1))
                if ch == 0:
                    V.wait_ge(s_v.h, V_INVS)
                if ch >= 2:
                    V.wait_ge(s_sp.h, wdsp[ch - 2])
                V.tensor_scalar(UQ1, buf, INVd, 1.0, OP.mult, OP.min)
                V.tensor_scalar(UQ2, UQ1, -1.0, CC, OP.max, OP.add)
                s_v.inc(V.tensor_scalar(qb, UQ2, CC, None, OP.subtract))
                SY.wait_ge(s_v.h, s_v.n)
                s_sp.dma(SY.dma_start(wdq_sp[ch * 128:(ch + 1) * 128, :], qb))
                wdsp.append(s_sp.n)
            WDQ_SP = s_sp.n
            s_row.dma(SY.dma_start(ra3, rows_sp[2:3, :]))
            V.wait_ge(s_row.h, s_row.n)
            V.wait_ge(s_row.h, ROW_SCALES)
            s_v.inc(V.tensor_scalar(rb3, ra3, SCg[0:1, :], None, OP.mult))
            SY.wait_ge(s_v.h, s_v.n)
            s_row.dma(SY.dma_start(rows_sp[3:4, :], rb3))
            SY.wait_ge(s_row.h, s_row.n)
            s_row.dma(SY.dma_start(bc1[:, :], rows_sp[3:4, :].broadcast_to([128, T])))
            ROW_ALPHA = s_row.n
            V.memset(MAXP(KI - 1 + 1), -1e30) if False else None
            s_v.inc(V.memset(MAXP(1), -1e30))
            s_v.inc(V.memset(MINN(1), 1e30))
            V.wait_ge(s_v.h, s_v.n)
            GUB = tmpC3[:, 0:T]
            UAB = tmpC3[:, D:2 * D]
            WDQ_V = s_v.n
            wg0 = s_wld.n
            q_done = []
            gu_done = []
            pe_q = []
            gusp = []
            for ob in range(KI):
                if ob >= 1:
                    SQ.wait_ge(s_v.h, q_done[ob - 1])
                else:
                    SQ.wait_ge(s_v.h, WDQ_V)
                s_wld.dma(SQ.dma_start(wldF[:, 0:D].rearrange("p (k c) -> p k c", c=128),
                                       castsrc(wgT, ob)))
                s_wld.dma(SQ.dma_start(wldF[:, D:2 * D].rearrange("p (k c) -> p k c", c=128),
                                       castsrc(wuT, ob)))
                V.wait_ge(s_wld.h, wg0 + 32 * (ob + 1))
                if ob == 0:
                    V.wait_ge(s_ld.h, XQ_LD)
                    V.wait_ge(s_row.h, ROW_ALPHA)
                if ob >= 1:
                    V.wait_ge(s_pe.h, pe_q[ob - 1])
                    V.wait_ge(s_sp.h, gusp[ob - 1])
                V.tensor_scalar(UQ1, wldF[:, 0:D], INVg, 1.0, OP.mult, OP.min)
                V.tensor_scalar(UQ2, UQ1, -1.0, CC, OP.max, OP.add)
                s_v.inc(V.tensor_scalar(wqB[:, 0:D], UQ2, CC, None, OP.subtract))
                V.tensor_scalar(UQ1, wldF[:, D:2 * D], INVu, 1.0, OP.mult, OP.min)
                V.tensor_scalar(UQ2, UQ1, -1.0, CC, OP.max, OP.add)
                s_v.inc(V.tensor_scalar(wqB[:, D:2 * D], UQ2, CC, None, OP.subtract))
                q_done.append(s_v.n)
                wqg = wqB[:, 0:D].rearrange("p (k c) -> p k c", c=128)
                wqu = wqB[:, D:2 * D].rearrange("p (k c) -> p k c", c=128)
                PE.wait_ge(s_v.h, q_done[ob])
                if ob >= 1:
                    PE.wait_ge(s_v.h, gu_done[ob - 1])
                for tw in range(NW):
                    for k in range(NB):
                        PE.matmul(BANK(tw), wqg[:, k, :], xnT(k)[:, tw * W:(tw + 1) * W],
                                  start=(k == 0), stop=(k == NB - 1))
                for tw in range(NW):
                    for k in range(NB):
                        mm = PE.matmul(BANK(4 + tw), wqu[:, k, :], xnT(k)[:, tw * W:(tw + 1) * W],
                                       start=(k == 0), stop=(k == NB - 1))
                s_pe.inc(mm)
                pe_q.append(s_pe.n)
                V.wait_ge(s_pe.h, pe_q[ob])
                for tw in range(NW):
                    s_v.inc(V.tensor_mul(UAB[:, tw * W:(tw + 1) * W], BANK(tw),
                                         bc1[:, tw * W:(tw + 1) * W]))
                S.wait_ge(s_v.h, s_v.n)
                s_s.inc(S.activation(sgb[:, :], UAB, AF.Silu))
                V.wait_ge(s_s.h, s_s.n)
                for tw in range(NW):
                    V.tensor_mul(GUB[:, tw * W:(tw + 1) * W], sgb[:, tw * W:(tw + 1) * W],
                                 BANK(4 + tw))
                V.scalar_tensor_tensor(MAXP(ob), GUB, 1.0, MAXP(ob + 1), OP.mult, OP.max)
                s_v.inc(V.scalar_tensor_tensor(MINN(ob), GUB, 1.0, MINN(ob + 1), OP.mult, OP.min))
                gu_done.append(s_v.n)
                SY.wait_ge(s_v.h, gu_done[ob])
                s_sp.dma(SY.dma_start(gu_sp[ob * 128:(ob + 1) * 128, :], GUB))
                gusp.append(s_sp.n)
            GU_SP = s_sp.n
            fp = (KI - 1) % 2
            V.tensor_scalar(MINN(1 - fp), MINN(fp), -1.0, None, OP.mult)
            s_v.inc(V.scalar_tensor_tensor(MAXP(1 - fp), MINN(1 - fp), 1.0, MAXP(fp),
                                           OP.mult, OP.max))
            top4 = part_max_tree(MAXP(1 - fp), MAXP(fp), MINN(fp))
            s_v.inc(V.tensor_copy(rb3, top4[0:1, :]))
            s_row.dma(SY.dma_start(R0, rows_sp[2:3, :]))
            V.wait_ge(s_row.h, s_row.n)
            V.tensor_scalar(R1, R0, SCu[0:1, :], None, OP.mult)
            V.tensor_mul(ra3, rb3, R1)
            V.tensor_scalar(ra3, ra3, INV127[0:1, :], EPS, OP.mult, OP.add)
            V.reciprocal(rb3, ra3)
            V.tensor_mul(rb3, R1, rb3)
            s_v.inc(V.tensor_scalar(ra3, ra3, SCd[0:1, :], None, OP.mult))
            SY.wait_ge(s_v.h, s_v.n)
            s_row.dma(SY.dma_start(rows_sp[3:4, :], rb3))
            s_row.dma(SY.dma_start(rows_sp[4:5, :], ra3))
            SY.wait_ge(s_row.h, s_row.n)
            s_row.dma(SY.dma_start(bc0[:, :], rows_sp[3:4, :].broadcast_to([128, T])))
            s_row.dma(SY.dma_start(bc1[:, :], rows_sp[4:5, :].broadcast_to([128, T])))
            ROW_INV2 = s_row.n
            P4_V = s_v.n
        stP5 = ExitStack()
        wdqB = stP5.enter_context(nc.sbuf_tensor("wdqB", [128, 2 * KI * 128], BF16))
        with stP5:
            q2q = lambda k: actA[:, k * WQ:(k + 1) * WQ]
            SY.wait_ge(s_sp.h, s_sp.n)
            SY.wait_ge(s_v.h, P4_V)
            GUL = lambda k: tmpC1[:, (k % 2) * WQ:(k % 2) * WQ + WQ]
            UQ5 = tmpC1[:, 2 * WQ:3 * WQ]
            UD1 = tmpC1[:, 3 * WQ:4 * WQ]
            UD2 = tmpC1[:, 2 * WQ:3 * WQ]
            OAC = lambda db: tmpC1[:, (4 + db % 2) * WQ:(4 + db % 2) * WQ + WQ]
            XRE = lambda db: tmpC1[:, (6 + db % 2) * WQ:(6 + db % 2) * WQ + WQ]
            dn_pe_prev = []
            for tq in range(TQ):
                gq0 = s_ld.n
                vq = []
                for k in range(KI):
                    if k >= 2:
                        SY.wait_ge(s_v.h, vq[k - 2])
                    s_ld.dma(SY.dma_start(GUL(k), gu_sp[k * 128:(k + 1) * 128, tq * WQ:(tq + 1) * WQ]))
                    V.wait_ge(s_ld.h, gq0 + 16 * (k + 1))
                    if k == 0:
                        V.wait_ge(s_row.h, ROW_INV2)
                        if tq > 0:
                            V.wait_ge(s_pe.h, dn_pe_prev[-1])
                        else:
                            V.wait_ge(s_pe.h, s_pe.n)
                    V.tensor_mul(UQ5, GUL(k), bc0[:, tq * WQ:(tq + 1) * WQ])
                    s_v.inc(V.tensor_scalar(q2q(k), UQ5, CC, CC, OP.add, OP.subtract))
                    vq.append(s_v.n)
                Q2_V = s_v.n
                wd5 = s_ld.n
                dn_pe = []
                dn_v = []
                for db in range(NB):
                    wslot = wdqB[:, (db % 2) * KI * 128:((db % 2) + 1) * KI * 128]
                    if db >= 2:
                        SY.wait_ge(s_pe.h, dn_pe[db - 2])
                        SY.wait_ge(s_v.h, dn_v[db - 2])
                    s_ld.dma(SY.dma_start(
                        wslot.rearrange("p (k c) -> p k c", c=128),
                        wdq_sp.rearrange("(k p) (o c) -> p k o c", p=128, c=128)[:, :, db, :]))
                    s_ld.dma(SY.dma_start(OAC(db), oacc_sp[db * 128:(db + 1) * 128,
                                                           tq * WQ:(tq + 1) * WQ]))
                    s_ld.dma(SY.dma_start(XRE(db), xT[db * 128:(db + 1) * 128,
                                                      tq * WQ:(tq + 1) * WQ]))
                    PE.wait_ge(s_ld.h, wd5 + 48 * (db + 1))
                    if db == 0:
                        PE.wait_ge(s_v.h, Q2_V)
                        if tq == 0:
                            PE.wait_ge(s_sp.h, WDQ_SP)
                    if db >= 2:
                        PE.wait_ge(s_v.h, dn_v[db - 2])
                    wv = wslot.rearrange("p (k c) -> p k c", c=128)
                    for k in range(KI):
                        mm = PE.matmul(BANK(db % 2), wv[:, k, :], q2q(k),
                                       start=(k == 0), stop=(k == KI - 1))
                    s_pe.inc(mm)
                    dn_pe.append(s_pe.n)
                    V.wait_ge(s_pe.h, dn_pe[db])
                    V.wait_ge(s_fin.h, s_fin.n)
                    V.tensor_mul(UD1, BANK(db % 2), bc1[:, tq * WQ:(tq + 1) * WQ])
                    V.scalar_tensor_tensor(UD2, UD1, 1.0, OAC(db), OP.mult, OP.add)
                    s_v.inc(V.scalar_tensor_tensor(UD1, UD2, 1.0, XRE(db), OP.mult, OP.add))
                    dn_v.append(s_v.n)
                    SY.wait_ge(s_v.h, dn_v[db])
                    s_fin.dma(SY.dma_start(outT[db * 128:(db + 1) * 128, tq * WQ:(tq + 1) * WQ],
                                           UD1))
                dn_pe_prev = dn_pe
            SY.wait_ge(s_fin.h, s_fin.n)
    return nc


def kernel(L0_attn_x, L0_attn_Wr, L0_attn_Wf, L0_attn_Wc, L0_attn_Wo,
           L0_attn_ln_scale, L0_ffn_w_gate, L0_ffn_w_up, L0_ffn_w_down):
    x = np.ascontiguousarray(np.asarray(L0_attn_x, dtype=np.float32))
    B = x.shape[0]
    shared = dict(
        Wr=np.ascontiguousarray(np.asarray(L0_attn_Wr, dtype=np.float32)),
        Wf=np.ascontiguousarray(np.asarray(L0_attn_Wf, dtype=np.float32)),
        Wc=np.ascontiguousarray(np.asarray(L0_attn_Wc, dtype=np.float32)),
        Wo=np.ascontiguousarray(np.asarray(L0_attn_Wo, dtype=np.float32)),
        lns=np.ascontiguousarray(np.asarray(L0_attn_ln_scale, dtype=np.float32).reshape(16, 128).T),
        wgT=np.ascontiguousarray(np.asarray(L0_ffn_w_gate, dtype=np.float32).T),
        wuT=np.ascontiguousarray(np.asarray(L0_ffn_w_up, dtype=np.float32).T),
        wdT=np.ascontiguousarray(np.asarray(L0_ffn_w_down, dtype=np.float32).T),
    )
    in_maps = [dict(shared, xT=np.ascontiguousarray(x[b].T)) for b in range(B)]
    nc = build()
    res = run_bass_kernel_spmd(nc, in_maps, core_ids=list(range(B)))
    out = np.stack([np.asarray(r["outT"], dtype=np.float32).T for r in res.results], axis=0)
    return np.ascontiguousarray(out)


# version 13
# speedup vs baseline: 1.1224x; 1.1224x over previous
from contextlib import ExitStack
import numpy as np
import ml_dtypes
import concourse.bass as bass
import concourse.mybir as mybir
from concourse.bass_utils import run_bass_kernel_spmd

F32 = mybir.dt.float32
BF16 = mybir.dt.bfloat16
FP16 = mybir.dt.float16
I32 = mybir.dt.int32
AF = mybir.ActivationFunctionType
OP = mybir.AluOpType
AX = mybir.AxisListType

D = 2048
T = 2048
HID = 8192
NB = D // 128
KI = HID // 128
W = 512
NW = T // W
TQ = 4
WQ = T // TQ
EPS = 1e-5
C32 = float(np.float32(1.5 * 2 ** 23))
C52 = float(np.float32(1.5 * 2 ** 52))


class Sem:
    def __init__(self, nc, st, name):
        self.h = st.enter_context(nc.semaphore(name))
        self.n = 0

    def inc(self, instr, k=1):
        instr.then_inc(self.h, k)
        self.n += k
        return instr

    def dma(self, instr):
        return self.inc(instr, 16)


def build(dbg=False, dbgwin=None):
    if dbgwin is None:
        dbgwin = dbg
    nc = bass.Bass("TRN2", target_bir_lowering=False, detect_race_conditions=False)

    xT = nc.dram_tensor("xT", [D, T], F32, kind="ExternalInput").ap()
    Wr = nc.dram_tensor("Wr", [D, D], F32, kind="ExternalInput").ap()
    Wf = nc.dram_tensor("Wf", [D, D], F32, kind="ExternalInput").ap()
    Wc = nc.dram_tensor("Wc", [D, D], F32, kind="ExternalInput").ap()
    Wo = nc.dram_tensor("Wo", [D, D], F32, kind="ExternalInput").ap()
    lns = nc.dram_tensor("lns", [128, NB], F32, kind="ExternalInput").ap()
    wgT = nc.dram_tensor("wgT", [D, HID], F32, kind="ExternalInput").ap()
    wuT = nc.dram_tensor("wuT", [D, HID], F32, kind="ExternalInput").ap()
    wdT = nc.dram_tensor("wdT", [HID, D], F32, kind="ExternalInput").ap()
    outT = nc.dram_tensor("outT", [D, T], F32, kind="ExternalOutput").ap()

    KIND = "ExternalOutput" if dbg else "Internal"
    if dbg:
        f_dbg = nc.dram_tensor("f_dbg", [D, T], F32, kind="ExternalOutput").ap()
        c_dbg = nc.dram_tensor("c_dbg", [D, T], F32, kind="ExternalOutput").ap()
        h_dbg = nc.dram_tensor("h_dbg", [D, T], F32, kind="ExternalOutput").ap()
        r_dbg = nc.dram_tensor("r_dbg", [D, T], F32, kind="ExternalOutput").ap()
        g_dbg = nc.dram_tensor("g_dbg", [D, T], BF16, kind="ExternalOutput").ap()
    xq_sp = nc.dram_tensor("xq_sp", [D, T], BF16, kind=KIND).ap()
    gu_sp = nc.dram_tensor("gu_sp", [HID, T], F32, kind=KIND).ap()
    wdq_sp = nc.dram_tensor("wdq_sp", [HID, D], BF16, kind=KIND).ap()
    oacc_sp = nc.dram_tensor("oacc_sp", [D, T], F32, kind=KIND).ap()
    rows_sp = nc.dram_tensor("rows_sp", [8, T], F32, kind=KIND).ap()

    st = ExitStack()
    sbt = lambda n_, sh, dt: st.enter_context(nc.sbuf_tensor(n_, sh, dt))
    with st:
        actA = sbt("actA", [128, NB * T], BF16)
        tmpC1 = sbt("tmpC1", [128, 2 * T], F32)
        rows = sbt("rows", [1, 2 * T], F32)
        bc0 = sbt("bc0", [128, T], F32)
        bc1 = sbt("bc1", [128, T], F32)
        cols = sbt("cols", [128, 64], F32)
        d_i32 = sbt("d_i32", [128, 1], I32)
        scaf = sbt("scaf", [128, 32], F32)
        psum = st.enter_context(nc.psum_tensor("ps", [128, 4096], F32))
        BANK = lambda b: psum[:, b * 512:(b + 1) * 512]
        PHALF = lambda h: psum[:, h * 2048:h * 2048 + 2048]

        CC = cols[:, 0:1]
        LNSC = lambda b: cols[:, 1 + b:2 + b]
        CAR = lambda b: cols[:, 17 + b:18 + b]
        INV127 = cols[:, 33:34]
        R0 = rows[:, 0:T]
        R1 = rows[:, T:2 * T]

        s_ld = Sem(nc, st, "s_ld")
        s_wld = Sem(nc, st, "s_wld")
        s_g = Sem(nc, st, "s_g")
        s_v = Sem(nc, st, "s_v")
        s_s = Sem(nc, st, "s_s")
        s_pe = Sem(nc, st, "s_pe")
        s_cast = Sem(nc, st, "s_cast")
        s_row = Sem(nc, st, "s_row")
        s_sp = Sem(nc, st, "s_sp")
        s_fin = Sem(nc, st, "s_fin")

        V, S, G, PE, SY = nc.vector, nc.scalar, nc.gpsimd, nc.tensor, nc.sync
        SQ = nc.scalar

        def part_max_tree(a, b, c, psum_src=None):
            if psum_src is not None:
                s_v.inc(V.tensor_copy(a, psum_src))
            tri = [a, b, c]
            cur = 0
            for n in (64, 32, 16, 8, 4, 2, 1):
                aa, bb, cc = tri[cur % 3], tri[(cur + 1) % 3], tri[(cur + 2) % 3]
                s_v.inc(V.nop()) if False else None
                SY.wait_ge(s_v.h, s_v.n)
                s_row.dma(SY.dma_start(bb[0:n, :], aa[n:2 * n, :]))
                V.wait_ge(s_row.h, s_row.n)
                s_v.inc(V.scalar_tensor_tensor(cc[0:n, :], bb[0:n, :], 1.0, aa[0:n, :],
                                               OP.mult, OP.max))
                cur += 2
            return tri[cur % 3]

        s_v.inc(V.memset(scaf[:, 0:1], 0.7))
        V.wait_ge(s_v.h, s_v.n)
        s_v.inc(V.tensor_scalar(d_i32[:, :], scaf[:, 0:1], 0.0, None, OP.add))
        G.wait_ge(s_v.h, s_v.n)
        s_cast.dma(G.dma_start(scaf[:, 1:2], d_i32[:, :]))
        V.wait_ge(s_cast.h, s_cast.n)
        V.tensor_scalar(scaf[:, 2:3], scaf[:, 1:2], C32, None, OP.mult)
        V.tensor_scalar(scaf[:, 3:4], scaf[:, 1:2], -C52, C52, OP.mult, OP.add)
        V.tensor_add(CC, scaf[:, 2:3], scaf[:, 3:4])
        V.tensor_scalar(cols[:, 34:35], scaf[:, 1:2], -3e-7, 1.0, OP.mult, OP.add)
        s_v.inc(V.memset(scaf[:, 14:15], 1.0))
        ONES = scaf[:, 14:15]
        s_v.inc(V.memset(scaf[:, 15:16], 127.0))
        V.wait_ge(s_v.h, s_v.n)
        s_v.inc(V.reciprocal(INV127, scaf[:, 15:16]))
        V_CONST = s_v.n
        s_ld.dma(SY.dma_start(cols[:, 1:17], lns))
        LNS_LD = s_ld.n

        stP12 = ExitStack()
        gld = stP12.enter_context(nc.sbuf_tensor("gld", [128, 2 * 2048], F32))
        junk2 = stP12.enter_context(nc.sbuf_tensor("junk2", [128, 2048], F32))
        tmpC2 = stP12.enter_context(nc.sbuf_tensor("tmpC2", [128, 2 * T], F32))
        xstg = stP12.enter_context(nc.sbuf_tensor("xstg", [128, T], BF16))
        with stP12:
            p1chunks = []
            for mat, nrow, wid, slot in ((wgT, NB, HID, 4), (wuT, NB, HID, 5), (wdT, KI, D, 6)):
                for rchunk in range(nrow):
                    for c0 in range(0, wid, 2048):
                        p1chunks.append((mat, rchunk, c0, 2048, slot))
            accslot = {4: (4, 20), 5: (5, 21), 6: (6, 22)}
            nacc = {4: 0, 5: 0, 6: 0}
            gdone = []
            sdone = []
            for ci, (mat, rc, c0, wid, slot) in enumerate(p1chunks):
                buf = gld[:, (ci % 2) * 2048:(ci % 2) * 2048 + wid]
                if ci >= 2:
                    SQ.wait_ge(s_g.h, gdone[ci - 2])
                s_wld.dma(SQ.dma_start(buf, mat[rc * 128:(rc + 1) * 128, c0:c0 + wid]))
                S.wait_ge(s_wld.h, 16 * (ci + 1))
                par = scaf[:, 8 + ci % 2:9 + ci % 2]
                if ci >= 2:
                    S.wait_ge(s_g.h, gdone[ci - 1])
                s_s.inc(S.activation(junk2[:, 0:wid], buf, AF.Abs, accum_out=par))
                sdone.append(s_s.n)
                G.wait_ge(s_s.h, sdone[ci])
                a, b = accslot[slot]
                k = nacc[slot]
                if k == 0:
                    s_g.inc(G.tensor_copy(scaf[:, a:a + 1], par))
                else:
                    psrc = scaf[:, (a if k % 2 == 1 else b):(a if k % 2 == 1 else b) + 1]
                    pdst = scaf[:, (b if k % 2 == 1 else a):(b if k % 2 == 1 else a) + 1]
                    s_g.inc(G.tensor_tensor(pdst, par, psrc, OP.add))
                nacc[slot] += 1
                gdone.append(s_g.n)
            for slot in (4, 5, 6):
                a, b = accslot[slot]
                fin = a if (nacc[slot] % 2 == 1) else b
                if fin != slot:
                    s_g.inc(G.tensor_copy(scaf[:, slot:slot + 1], scaf[:, fin:fin + 1]))
            G_PARTIALS = s_g.n
            SCg, SCu, SCd = scaf[:, 16:17], scaf[:, 17:18], scaf[:, 18:19]
            INVg, INVu, INVd = scaf[:, 10:11], scaf[:, 11:12], scaf[:, 12:13]

            xblk = lambda b: xT[b * 128:(b + 1) * 128, :]
            XA = lambda b: tmpC1[:, (b % 2) * T:(b % 2) * T + T]
            XSQ = lambda b: tmpC2[:, (b % 2) * T:(b % 2) * T + T]
            pa0 = s_ld.n
            pe_blk = []
            for b in range(NB):
                if b >= 2:
                    SY.wait_ge(s_pe.h, pe_blk[b - 2])
                s_ld.dma(SY.dma_start(XA(b), xblk(b)))
                PE.wait_ge(s_ld.h, pa0 + 16 * (b + 1))
                if b == 0:
                    PE.wait_ge(s_v.h, V_CONST)
                for w in range(NW):
                    mm = PE.matmul(psum[0:1, w * W:(w + 1) * W], ONES, XA(b)[:, w * W:(w + 1) * W],
                                   start=(b == 0), stop=(b == NB - 1))
                s_pe.inc(mm)
                pe_blk.append(s_pe.n)
            ra = tmpC2[0:1, 0:T]
            rb = tmpC2[0:1, T:2 * T]
            V.wait_ge(s_pe.h, s_pe.n)
            V.tensor_scalar(R0, psum[0:1, 0:2048], 1.0 / D, None, OP.mult)
            s_v.inc(V.tensor_scalar(R1, R0, -1.0, None, OP.mult))
            SY.wait_ge(s_v.h, s_v.n)
            s_row.dma(SY.dma_start(rows_sp[1:2, :], R1))
            SY.wait_ge(s_row.h, s_row.n)
            s_row.dma(SY.dma_start(bc0[:, :], rows_sp[1:2, :].broadcast_to([128, T])))
            ROW_NMU = s_row.n
            pa1 = s_ld.n
            pe_blk2 = []
            v_blk0 = []
            ss2 = []
            for b in range(NB):
                if b >= 2:
                    SY.wait_ge(s_pe.h, pe_blk2[b - 2])
                s_ld.dma(SY.dma_start(XA(b), xblk(b)))
                V.wait_ge(s_ld.h, pa1 + 16 * (b + 1))
                if b == 0:
                    V.wait_ge(s_row.h, ROW_NMU)
                if b >= 2:
                    V.wait_ge(s_s.h, ss2[b - 2])
                s_v.inc(V.tensor_add(XSQ(b), XA(b), bc0[:, :]))
                v_blk0.append(s_v.n)
                S.wait_ge(s_v.h, v_blk0[b])
                if b >= 2:
                    S.wait_ge(s_pe.h, pe_blk2[b - 2])
                s_s.inc(S.activation(XA(b), XSQ(b), AF.Square))
                ss2.append(s_s.n)
                PE.wait_ge(s_s.h, s_s.n)
                for w in range(NW):
                    mm = PE.matmul(psum[0:1, 2048 + w * W:2048 + (w + 1) * W], ONES,
                                   XA(b)[:, w * W:(w + 1) * W], start=(b == 0), stop=(b == NB - 1))
                s_pe.inc(mm)
                pe_blk2.append(s_pe.n)
            V.wait_ge(s_pe.h, s_pe.n)
            s_v.inc(V.tensor_scalar(rb, psum[0:1, 2048:4096], 1.0 / D, EPS, OP.mult, OP.add))
            V_STATS = s_v.n
            S.wait_ge(s_v.h, s_v.n)
            s_s.inc(S.activation(ra, rb, AF.Sqrt))
            V.wait_ge(s_s.h, s_s.n)
            V.reciprocal(R0, ra)
            V.tensor_mul(ra, R0, R0)
            V.tensor_mul(ra, ra, rb)
            V.tensor_scalar(ra, ra, -0.5, 1.5, OP.mult, OP.add)
            V.tensor_mul(rb, R0, ra)
            V.tensor_mul(ra, R1, rb)
            V.tensor_copy(R1, rb)
            s_v.inc(V.tensor_copy(R0, ra))
            SY.wait_ge(s_v.h, s_v.n)
            s_row.dma(SY.dma_start(rows_sp[0:1, :], R1))
            s_row.dma(SY.dma_start(rows_sp[1:2, :], R0))
            SY.wait_ge(s_row.h, s_row.n)
            s_row.dma(SY.dma_start(bc0[:, :], rows_sp[0:1, :].broadcast_to([128, T])))
            s_row.dma(SY.dma_start(bc1[:, :], rows_sp[1:2, :].broadcast_to([128, T])))
            ROW_RS = s_row.n
            xnT = lambda b: actA[:, b * T:(b + 1) * T]
            UB1 = lambda b: tmpC2[:, (b % 2) * T:(b % 2) * T + T]
            s_v.inc(V.memset(PHALF(1), 0.0))
            V.wait_ge(s_v.h, s_v.n)
            pb0 = s_ld.n
            v_blk = []
            for b in range(NB):
                if b >= 2:
                    SY.wait_ge(s_v.h, v_blk[b - 2])
                s_ld.dma(SY.dma_start(XA(b), xblk(b)))
                V.wait_ge(s_ld.h, pb0 + 16 * (b + 1))
                if b == 0:
                    V.wait_ge(s_row.h, ROW_RS)
                u = UB1(b)
                V.tensor_mul(u, XA(b), bc0[:, :])
                V.scalar_tensor_tensor(xnT(b), u, 1.0, bc1[:, :], OP.mult, OP.add)
                V.scalar_tensor_tensor(XA(b), u, 1.0, bc1[:, :], OP.mult, OP.add)
                V.scalar_tensor_tensor(u, XA(b), -1.0, XA(b), OP.mult, OP.max)
                s_v.inc(V.scalar_tensor_tensor(PHALF(b % 2), u, 1.0, PHALF(1 - b % 2),
                                               OP.mult, OP.max))
                v_blk.append(s_v.n)
            TRA = tmpC2[:, 0:T]
            TRB = tmpC2[:, T:2 * T]
            TRC = bc0[:, :]
            TOPROW = part_max_tree(TRA, TRB, TRC, psum_src=PHALF(1))
            V_PSUMFREE = s_v.n
            V.tensor_scalar(R0, TOPROW[0:1, :], INV127[0:1, :], EPS, OP.mult, OP.add)
            s_v.inc(V.reciprocal(R1, R0))
            SY.wait_ge(s_v.h, s_v.n)
            s_row.dma(SY.dma_start(rows_sp[2:3, :], R0))
            SY.wait_ge(s_row.h, s_row.n)
            s_row.dma(SY.dma_start(tmpC2[0:1, 0:T], rows_sp[0:1, :]))
            s_row.dma(SY.dma_start(tmpC2[0:1, T:2 * T], rows_sp[1:2, :]))
            V.wait_ge(s_row.h, s_row.n)
            V.tensor_mul(R0, ra, R1)
            V.tensor_mul(ra, rb, R1)
            s_v.inc(V.tensor_copy(R1, ra))
            SY.wait_ge(s_v.h, s_v.n)
            s_row.dma(SY.dma_start(rows_sp[4:5, :], R0))
            s_row.dma(SY.dma_start(rows_sp[5:6, :], R1))
            SY.wait_ge(s_row.h, s_row.n)
            s_row.dma(SY.dma_start(bc0[:, :], rows_sp[4:5, :].broadcast_to([128, T])))
            s_row.dma(SY.dma_start(bc1[:, :], rows_sp[5:6, :].broadcast_to([128, T])))
            ROW_RQ = s_row.n
            pc0 = s_ld.n
            v_blk2 = []
            sp_blk2 = []
            for b in range(NB):
                if b >= 2:
                    SY.wait_ge(s_v.h, v_blk2[b - 2])
                s_ld.dma(SY.dma_start(XA(b), xblk(b)))
                V.wait_ge(s_ld.h, pc0 + 16 * (b + 1))
                if b == 0:
                    V.wait_ge(s_row.h, ROW_RQ)
                    V.wait_ge(s_v.h, V_CONST)
                if b >= 1:
                    V.wait_ge(s_sp.h, sp_blk2[b - 1])
                u = UB1(b)
                V.tensor_mul(u, XA(b), bc0[:, :])
                V.scalar_tensor_tensor(XA(b), u, 1.0, bc1[:, :], OP.mult, OP.add)
                s_v.inc(V.tensor_scalar(xstg[:, :], XA(b), CC, CC, OP.add, OP.subtract))
                v_blk2.append(s_v.n)
                SY.wait_ge(s_v.h, v_blk2[b])
                s_sp.dma(SY.dma_start(xq_sp[b * 128:(b + 1) * 128, :], xstg[:, :]))
                sp_blk2.append(s_sp.n)
            XQ_SPILLED = s_sp.n
        stP3 = ExitStack()
        actB = stP3.enter_context(nc.sbuf_tensor("actB", [128, NB * T], BF16))
        wbuf = stP3.enter_context(nc.sbuf_tensor("wbuf", [128, 6 * NB * 128], BF16))
        heven = stP3.enter_context(nc.sbuf_tensor("heven", [128, T // 2], F32))
        pwin = stP3.enter_context(nc.sbuf_tensor("pwin", [128, W], BF16))
        with stP3:
            WSL = lambda s: wbuf[:, s * NB * 128:(s + 1) * NB * 128].rearrange("p (k c) -> p k c", c=128)
            castsrc = lambda mat, ob: mat.rearrange("(k p) (o c) -> p k o c", p=128, c=128)[:, :, ob, :]
            Gblk = lambda b: actB[:, b * T:(b + 1) * T]
            fbF = tmpC1[:, 0:T]
            cbF = tmpC1[:, T:2 * T]
            _pyr = {}
            off0 = 0
            for k in range(1, 12):
                n = T >> k
                _pyr[k] = n
            pyr_f = {1: bc0[:, 0:1024], 2: bc1[:, 0:512]}
            pyr_c = {1: bc0[:, 1024:2048], 2: bc1[:, 512:1024]}
            off = 1024
            for k in range(3, 12):
                n = T >> k
                pyr_f[k] = bc1[:, off:off + n]; off += n
                pyr_c[k] = bc1[:, off:off + n]; off += n
            lev_f = {0: fbF}
            lev_c = {0: cbF}
            lev_f.update(pyr_f)
            lev_c.update(pyr_c)
            PS_T3 = lambda n: psum[:, 0:n]
            PS_T4 = lambda n: psum[:, 2048:2048 + n]

            def vbar(ins):
                s_v.inc(ins)
                V.wait_ge(s_v.h, s_v.n)
                return ins
            V.wait_ge(s_ld.h, LNS_LD)
            G.wait_ge(s_wld.h, s_wld.n)
            G.wait_ge(s_sp.h, XQ_SPILLED)
            G.wait_ge(s_v.h, s_v.n)
            G.wait_ge(s_row.h, s_row.n)
            cast_ob = []
            for ob in range(2):
                for mi, mat in enumerate((Wf, Wc, Wr)):
                    s_cast.dma(G.dma_start(WSL(3 * (ob % 2) + mi), castsrc(mat, ob)))
                cast_ob.append(s_cast.n)
            prev_s = s_s.n
            prev_v = s_v.n
            rowacc_v = []
            for ob in range(NB):
                sl = 3 * (ob % 2)
                PE.wait_ge(s_cast.h, cast_ob[ob])
                PE.wait_ge(s_s.h, prev_s)
                PE.wait_ge(s_v.h, prev_v)
                if ob == 0:
                    PE.wait_ge(s_v.h, V_PSUMFREE)
                for tw in range(NW):
                    for k in range(NB):
                        PE.matmul(BANK(tw), WSL(sl + 0)[:, k, :], xnT(k)[:, tw * W:(tw + 1) * W],
                                  start=(k == 0), stop=(k == NB - 1))
                for tw in range(NW):
                    for k in range(NB):
                        mm = PE.matmul(BANK(4 + tw), WSL(sl + 1)[:, k, :], xnT(k)[:, tw * W:(tw + 1) * W],
                                       start=(k == 0), stop=(k == NB - 1))
                s_pe.inc(mm)
                PE_FC = s_pe.n
                S.wait_ge(s_pe.h, PE_FC)
                for tw in range(NW):
                    S.activation(fbF[:, tw * W:(tw + 1) * W], BANK(tw), AF.Sigmoid)
                for tw in range(NW):
                    ins_ = S.activation(cbF[:, tw * W:(tw + 1) * W], BANK(4 + tw), AF.Silu)
                s_s.inc(ins_)
                S_FC = s_s.n
                if dbgwin:
                    SY.wait_ge(s_s.h, S_FC)
                    s_sp.dma(SY.dma_start(f_dbg[ob * 128:(ob + 1) * 128, :], fbF))
                    s_sp.dma(SY.dma_start(c_dbg[ob * 128:(ob + 1) * 128, :], cbF))
                    SY_DBG_FC = s_sp.n
                V.wait_ge(s_s.h, S_FC)
                for k in range(0, 11):
                    n = T >> (k + 1)
                    sf, sc = lev_f[k], lev_c[k]
                    df, dc = lev_f[k + 1], lev_c[k + 1]
                    fa = sf[:, 0:2 * n:2]
                    fb = sf[:, 1:2 * n:2]
                    ca = sc[:, 0:2 * n:2]
                    cb = sc[:, 1:2 * n:2]
                    vbar(V.tensor_mul(df, fa, fb))
                    vbar(V.tensor_sub(PS_T3(n), ca, cb))
                    vbar(V.tensor_mul(heven[:, 0:n], PS_T3(n), fb))
                    vbar(V.tensor_add(dc, heven[:, 0:n], cb))
                for k in range(10, 0, -1):
                    n = T >> k
                    half = n // 2
                    of = lev_f[k + 1]
                    oc = lev_c[k + 1]
                    ef_src = lev_f[k]
                    ec_src = lev_c[k]
                    m = half - 1
                    if m > 0:
                        xf = ef_src[:, 2:n:2]
                        xc = ec_src[:, 2:n:2]
                        vbar(V.tensor_mul(PS_T4(m), of[:, 0:m], xf))
                        vbar(V.tensor_sub(PS_T3(m), oc[:, 0:m], xc))
                        vbar(V.tensor_mul(heven[:, 0:m], PS_T3(m), xf))
                        vbar(V.tensor_add(psum[:, 1024:1024 + m], heven[:, 0:m], xc))
                        vbar(V.tensor_copy(ec_src[:, 2:n:2], psum[:, 1024:1024 + m]))
                        vbar(V.tensor_copy(ef_src[:, 2:n:2], PS_T4(m)))
                    vbar(V.tensor_copy(ec_src[:, 1:n:2], oc[:, 0:half]))
                    vbar(V.tensor_copy(ef_src[:, 1:n:2], of[:, 0:half]))
                m = T // 2 - 1
                xf0 = fbF[:, 2:T:2]
                xc0 = cbF[:, 2:T:2]
                vbar(V.tensor_sub(PS_T3(m), pyr_c[1][:, 0:m], xc0))
                vbar(V.tensor_mul(psum[:, 1024:1024 + m], PS_T3(m), xf0))
                vbar(V.tensor_add(heven[:, 1:T // 2], psum[:, 1024:1024 + m], xc0))
                vbar(V.tensor_copy(heven[:, 0:1], cbF[:, 0:1]))
                if dbgwin:
                    SY.wait_ge(s_v.h, s_v.n + 0) if False else None
                if dbgwin:
                    s_v.inc(V.nop()) if False else None
                    _hm = s_v.n
                    s_v.inc(V.tensor_copy(pwin[:, 0:1], heven[:, 0:1]))
                    SY.wait_ge(s_v.h, s_v.n)
                    s_sp.dma(SY.dma_start(h_dbg[ob * 128:(ob + 1) * 128, 0:T // 2], heven[:, :]))
                    s_sp.dma(SY.dma_start(h_dbg[ob * 128:(ob + 1) * 128, T // 2:T], pyr_c[1]))
                    SY.wait_ge(s_sp.h, s_sp.n)
                vbar(V.tensor_mul(fbF[:, 0:T:2], heven[:, :], heven[:, :]))
                vbar(V.tensor_mul(fbF[:, 1:T:2], pyr_c[1], pyr_c[1]))
                V_H2 = s_v.n
                PE.wait_ge(s_v.h, V_H2)
                for tw in range(NW):
                    mm = PE.matmul(psum[0:1, tw * W:(tw + 1) * W], ONES, fbF[:, tw * W:(tw + 1) * W],
                                   start=True, stop=True)
                s_pe.inc(mm)
                V.wait_ge(s_pe.h, s_pe.n)
                if ob == 0:
                    s_v.inc(V.tensor_copy(R0, psum[0:1, 0:2048]))
                else:
                    dst = R0 if ob % 2 == 0 else R1
                    srcr = R1 if ob % 2 == 0 else R0
                    s_v.inc(V.tensor_add(dst, psum[0:1, 0:2048], srcr))
                rowacc_v.append(s_v.n)
                PE.wait_ge(s_v.h, rowacc_v[-1])
                for tw in range(NW):
                    for k in range(NB):
                        mm = PE.matmul(BANK(tw), WSL(sl + 2)[:, k, :], xnT(k)[:, tw * W:(tw + 1) * W],
                                       start=(k == 0), stop=(k == NB - 1))
                s_pe.inc(mm)
                PE_R = s_pe.n
                if ob + 2 < NB:
                    G.wait_ge(s_pe.h, PE_R)
                    for mi, mat in enumerate((Wf, Wc, Wr)):
                        s_cast.dma(G.dma_start(WSL(3 * (ob % 2) + mi), castsrc(mat, ob + 2)))
                    cast_ob.append(s_cast.n)
                else:
                    cast_ob.append(s_cast.n)
                S.wait_ge(s_pe.h, PE_R)
                for tw in range(NW):
                    ins_ = S.activation(BANK(4 + tw), BANK(tw), AF.Silu)
                s_s.inc(ins_)
                prev_s = s_s.n
                V.wait_ge(s_s.h, prev_s)
                for tw in range(NW):
                    rw = BANK(4 + tw)
                    V.tensor_mul(pwin[:, 0:W:2], rw[:, 0:W:2], heven[:, tw * 256:(tw + 1) * 256])
                    vbar(V.tensor_mul(pwin[:, 1:W:2], rw[:, 1:W:2], pyr_c[1][:, tw * 256:(tw + 1) * 256]))
                    s_v.inc(V.tensor_scalar(Gblk(ob)[:, tw * W:(tw + 1) * W], pwin[:, :],
                                            LNSC(ob), None, OP.mult))
                prev_v = s_v.n
                if dbgwin:
                    SY.wait_ge(s_v.h, prev_v)
                    s_sp.dma(SY.dma_start(g_dbg[ob * 128:(ob + 1) * 128, :], Gblk(ob)))
                    s_sp.dma(SY.dma_start(r_dbg[ob * 128:(ob + 1) * 128, :], fbF))
                    SY.wait_ge(s_sp.h, s_sp.n)
            ra3 = tmpC1[0:1, 0:T]
            rb3 = tmpC1[0:1, T:2 * T]
            MSROW = R1
            s_v.inc(V.tensor_scalar(rb3, MSROW, 1.0 / D, EPS, OP.mult, OP.add))
            V_MSROW = s_v.n
            S.wait_ge(s_v.h, V_MSROW)
            s_s.inc(S.activation(ra3, rb3, AF.Sqrt))
            V.wait_ge(s_s.h, s_s.n)
            V.reciprocal(R0, ra3)
            V.tensor_mul(ra3, R0, R0)
            V.tensor_mul(ra3, ra3, rb3)
            V.tensor_scalar(ra3, ra3, -0.5, 1.5, OP.mult, OP.add)
            s_v.inc(V.tensor_mul(R1, R0, ra3))
            SY.wait_ge(s_v.h, s_v.n)
            s_row.dma(SY.dma_start(rows_sp[6:7, :], R1))
            SY.wait_ge(s_row.h, s_row.n)
            SY.wait_ge(s_v.h, s_v.n)
            s_row.dma(SY.dma_start(bc0[:, :], rows_sp[6:7, :].broadcast_to([128, T])))
            ROW_RSH = s_row.n
            OUTB = tmpC1[:, T:2 * T]
            V_GP = s_v.n
            oy_v = []
            oy_sp = []
            for ob in range(NB):
                G.wait_ge(s_pe.h, s_pe.n)
                s_cast.dma(G.dma_start(WSL(ob % 2), castsrc(Wo, ob)))
                PE.wait_ge(s_cast.h, s_cast.n)
                if ob == 0:
                    PE.wait_ge(s_v.h, V_GP)
                if ob >= 1:
                    PE.wait_ge(s_v.h, oy_v[ob - 1])
                for tw in range(NW):
                    if tw >= 1:
                        PE.wait_ge(s_v.h, s_v.n)
                    for k in range(NB):
                        mm = PE.matmul(BANK(7), WSL(ob % 2)[:, k, :],
                                       Gblk(k)[:, tw * W:(tw + 1) * W],
                                       start=(k == 0), stop=(k == NB - 1))
                    s_pe.inc(mm)
                    V.wait_ge(s_pe.h, s_pe.n)
                    if ob == 0 and tw == 0:
                        V.wait_ge(s_row.h, ROW_RSH)
                    if ob >= 1 and tw == 0:
                        V.wait_ge(s_sp.h, oy_sp[ob - 1])
                    s_v.inc(V.tensor_mul(OUTB[:, tw * W:(tw + 1) * W], BANK(7),
                                         bc0[:, tw * W:(tw + 1) * W]))
                oy_v.append(s_v.n)
                SY.wait_ge(s_v.h, oy_v[ob])
                s_sp.dma(SY.dma_start(oacc_sp[ob * 128:(ob + 1) * 128, :], OUTB))
                oy_sp.append(s_sp.n)
            OACC_SP = s_sp.n
            PE.wait_ge(s_g.h, G_PARTIALS)
            for i, slot in enumerate((4, 5, 6)):
                mm = PE.matmul(psum[0:1, 1536 + i:1537 + i], scaf[:, slot:slot + 1], ONES,
                               start=True, stop=True)
            s_pe.inc(mm)
            V.wait_ge(s_pe.h, s_pe.n)
            V.tensor_scalar(scaf[0:1, 20:23], psum[0:1, 1536:1539],
                            1.0 / (D * HID), EPS, OP.mult, OP.add)
            s_v.inc(V.tensor_scalar(scaf[0:1, 16:19], scaf[0:1, 20:23],
                                    cols[0:1, 34:35], None, OP.mult))
            SY.wait_ge(s_v.h, s_v.n)
            s_row.dma(SY.dma_start(rows_sp[7:8, 0:3], scaf[0:1, 16:19]))
            s_row.dma(SY.dma_start(rows_sp[7:8, 3:6], scaf[0:1, 0:3]))
            s_row.dma(SY.dma_start(rows_sp[7:8, 6:7], cols[0:1, 0:1]))
            SY.wait_ge(s_row.h, s_row.n)
            s_row.dma(SY.dma_start(scaf[:, 16:19], rows_sp[7:8, 0:3].broadcast_to([128, 3])))
            ROW_SCALES = s_row.n
            V.wait_ge(s_row.h, ROW_SCALES)
            s_v.inc(V.reciprocal(scaf[:, 10:13], scaf[:, 16:19]))
            V_INVS = s_v.n
            P3_PE = s_pe.n
            P3_V = s_v.n
        stP4 = ExitStack()
        sgb = stP4.enter_context(nc.sbuf_tensor("sgb", [128, 1024], F32))
        wldF = stP4.enter_context(nc.sbuf_tensor("wldF", [128, 2 * D], F32))
        wqB = stP4.enter_context(nc.sbuf_tensor("wqB", [128, 4 * D], BF16))
        mxb = stP4.enter_context(nc.sbuf_tensor("mxb", [128, 2 * T], F32))
        tmpC3 = stP4.enter_context(nc.sbuf_tensor("tmpC3", [128, 2 * D], F32))
        uab = stP4.enter_context(nc.sbuf_tensor("uab", [128, T], F32))
        with stP4:
            ACC = lambda i: mxb[:, (i % 2) * T:(i % 2) * T + T]
            UQ1 = tmpC3[:, 0:D]
            UQ2 = tmpC3[:, D:2 * D]
            SY.wait_ge(s_pe.h, P3_PE)
            SY.wait_ge(s_sp.h, s_sp.n)
            xq0 = s_ld.n
            for b in range(NB):
                s_ld.dma(SY.dma_start(xnT(b), xq_sp[b * 128:(b + 1) * 128, :]))
            XQ_LD = s_ld.n
            SQ.wait_ge(s_pe.h, P3_PE)
            wd0 = s_wld.n
            wdsp = []
            for ch in range(KI):
                buf = wldF[:, (ch % 2) * D:(ch % 2) * D + D]
                qb = wqB[:, (ch % 2) * D:(ch % 2) * D + D]
                if ch >= 2:
                    SQ.wait_ge(s_sp.h, wdsp[ch - 2])
                s_wld.dma(SQ.dma_start(buf, wdT[ch * 128:(ch + 1) * 128, :]))
                V.wait_ge(s_wld.h, wd0 + 16 * (ch + 1))
                if ch == 0:
                    V.wait_ge(s_v.h, V_INVS)
                if ch >= 2:
                    V.wait_ge(s_sp.h, wdsp[ch - 2])
                V.tensor_scalar(UQ1, buf, INVd, 1.0, OP.mult, OP.min)
                V.tensor_scalar(UQ2, UQ1, -1.0, CC, OP.max, OP.add)
                s_v.inc(V.tensor_scalar(qb, UQ2, CC, None, OP.subtract))
                SY.wait_ge(s_v.h, s_v.n)
                s_sp.dma(SY.dma_start(wdq_sp[ch * 128:(ch + 1) * 128, :], qb))
                wdsp.append(s_sp.n)
            WDQ_SP = s_sp.n
            s_row.dma(SY.dma_start(ra3, rows_sp[2:3, :]))
            V.wait_ge(s_row.h, s_row.n)
            V.wait_ge(s_row.h, ROW_SCALES)
            s_v.inc(V.tensor_scalar(rb3, ra3, SCg[0:1, :], None, OP.mult))
            SY.wait_ge(s_v.h, s_v.n)
            s_row.dma(SY.dma_start(rows_sp[3:4, :], rb3))
            SY.wait_ge(s_row.h, s_row.n)
            s_row.dma(SY.dma_start(bc1[:, :], rows_sp[3:4, :].broadcast_to([128, T])))
            ROW_ALPHA = s_row.n
            s_v.inc(V.memset(ACC(1), -1e30))
            V.wait_ge(s_v.h, s_v.n)
            GUBp = lambda i: tmpC3[:, (i % 2) * T:(i % 2) * T + T]
            qU1 = bc0[:, 0:D]
            qU2 = tmpC1[:, T:2 * T]
            WDQ_V = s_v.n
            wg0 = s_wld.n
            W2 = 1024

            def ldgu(ob):
                s_wld.dma(SQ.dma_start(wldF[:, 0:D].rearrange("p (k c) -> p k c", c=128),
                                       castsrc(wgT, ob)))
                s_wld.dma(SQ.dma_start(wldF[:, D:2 * D].rearrange("p (k c) -> p k c", c=128),
                                       castsrc(wuT, ob)))

            def quant3(dst, src, inv):
                V.tensor_scalar(qU1, src, inv, 1.0, OP.mult, OP.min)
                V.tensor_scalar(qU2, qU1, -1.0, CC, OP.max, OP.add)
                s_v.inc(V.tensor_scalar(dst, qU2, CC, None, OP.subtract))
                return s_v.n

            WQG = lambda i: wqB[:, (i % 2) * 2 * D:(i % 2) * 2 * D + D]
            WQU = lambda i: wqB[:, (i % 2) * 2 * D + D:(i % 2) * 2 * D + 2 * D]
            SQ.wait_ge(s_v.h, WDQ_V)
            ldgu(0)
            V.wait_ge(s_wld.h, wg0 + 32)
            V.wait_ge(s_row.h, ROW_ALPHA)
            quant3(WQG(0), wldF[:, 0:D], INVg)
            qmark = [quant3(WQU(0), wldF[:, D:2 * D], INVu)]
            peh = {}
            gub_done = {}
            silu_mark = {}
            gusp = []
            for ob in range(KI):
                if ob + 1 < KI:
                    SQ.wait_ge(s_v.h, qmark[ob])
                    ldgu(ob + 1)
                wqg = WQG(ob).rearrange("p (k c) -> p k c", c=128)
                wqu = WQU(ob).rearrange("p (k c) -> p k c", c=128)
                for h in (0, 1):
                    PE.wait_ge(s_v.h, qmark[ob])
                    if ob == 0 and h == 0:
                        PE.wait_ge(s_ld.h, XQ_LD)
                    if ob >= 1:
                        PE.wait_ge(s_v.h, gub_done[(ob - 1, h)])
                    for tw in (2 * h, 2 * h + 1):
                        for k in range(NB):
                            PE.matmul(BANK(tw), wqg[:, k, :], xnT(k)[:, tw * W:(tw + 1) * W],
                                      start=(k == 0), stop=(k == NB - 1))
                    for tw in (2 * h, 2 * h + 1):
                        for k in range(NB):
                            mm = PE.matmul(BANK(4 + tw), wqu[:, k, :],
                                           xnT(k)[:, tw * W:(tw + 1) * W],
                                           start=(k == 0), stop=(k == NB - 1))
                    s_pe.inc(mm)
                    peh[(ob, h)] = s_pe.n
                for h in (0, 1):
                    V.wait_ge(s_pe.h, peh[(ob, h)])
                    if ob == 0 and h == 0:
                        V.wait_ge(s_row.h, ROW_ALPHA)
                    if ob >= 1:
                        V.wait_ge(s_s.h, silu_mark[(ob - 1, h)])
                    for tw in (2 * h, 2 * h + 1):
                        ins = V.tensor_mul(uab[:, tw * W:(tw + 1) * W], BANK(tw),
                                           bc1[:, tw * W:(tw + 1) * W])
                    s_v.inc(ins)
                    S.wait_ge(s_v.h, s_v.n)
                    if ob + h >= 1:
                        prevh = (ob, 0) if h == 1 else (ob - 1, 1)
                        S.wait_ge(s_v.h, gub_done[prevh])
                    s_s.inc(S.activation(sgb[:, :], uab[:, h * W2:(h + 1) * W2], AF.Silu))
                    silu_mark[(ob, h)] = s_s.n
                    if ob + 1 < KI:
                        V.wait_ge(s_wld.h, wg0 + 32 * (ob + 2))
                        if h == 0:
                            quant3(WQG(ob + 1), wldF[:, 0:D], INVg)
                        else:
                            qmark.append(quant3(WQU(ob + 1), wldF[:, D:2 * D], INVu))
                    V.wait_ge(s_s.h, silu_mark[(ob, h)])
                    if h == 0 and ob >= 2:
                        V.wait_ge(s_sp.h, gusp[ob - 2])
                    for tw in (2 * h, 2 * h + 1):
                        ins = V.tensor_mul(GUBp(ob)[:, tw * W:(tw + 1) * W],
                                           sgb[:, (tw - 2 * h) * W:(tw - 2 * h + 1) * W],
                                           BANK(4 + tw))
                    s_v.inc(ins)
                    gub_done[(ob, h)] = s_v.n
                    gh = GUBp(ob)[:, h * W2:(h + 1) * W2]
                    tmp = bc0[:, h * W2:(h + 1) * W2]
                    V.scalar_tensor_tensor(tmp, gh, -1.0,
                                           ACC(ob + 1)[:, h * W2:(h + 1) * W2],
                                           OP.mult, OP.max)
                    s_v.inc(V.scalar_tensor_tensor(ACC(ob)[:, h * W2:(h + 1) * W2], gh, 1.0,
                                                   tmp, OP.mult, OP.max))
                G.wait_ge(s_v.h, gub_done[(ob, 1)])
                s_sp.dma(G.dma_start(gu_sp[ob * 128:(ob + 1) * 128, :], GUBp(ob)))
                gusp.append(s_sp.n)
            GU_SP = s_sp.n
            top4 = part_max_tree(ACC(KI - 1), ACC(KI), bc0)
            s_v.inc(V.tensor_copy(rb3, top4[0:1, :]))
            s_row.dma(SY.dma_start(R0, rows_sp[2:3, :]))
            V.wait_ge(s_row.h, s_row.n)
            V.tensor_scalar(R1, R0, SCu[0:1, :], None, OP.mult)
            V.tensor_mul(ra3, rb3, R1)
            V.tensor_scalar(ra3, ra3, INV127[0:1, :], EPS, OP.mult, OP.add)
            V.reciprocal(rb3, ra3)
            V.tensor_mul(rb3, R1, rb3)
            s_v.inc(V.tensor_scalar(ra3, ra3, SCd[0:1, :], None, OP.mult))
            SY.wait_ge(s_v.h, s_v.n)
            s_row.dma(SY.dma_start(rows_sp[3:4, :], rb3))
            s_row.dma(SY.dma_start(rows_sp[4:5, :], ra3))
            SY.wait_ge(s_row.h, s_row.n)
            s_row.dma(SY.dma_start(bc0[:, :], rows_sp[3:4, :].broadcast_to([128, T])))
            s_row.dma(SY.dma_start(bc1[:, :], rows_sp[4:5, :].broadcast_to([128, T])))
            ROW_INV2 = s_row.n
            P4_V = s_v.n
        stP5 = ExitStack()
        wdqB = stP5.enter_context(nc.sbuf_tensor("wdqB", [128, 2 * KI * 128], BF16))
        glb = stP5.enter_context(nc.sbuf_tensor("glb", [128, 2 * WQ], F32))
        with stP5:
            q2q = lambda k: actA[:, k * WQ:(k + 1) * WQ]
            SY.wait_ge(s_sp.h, s_sp.n)
            SQ.wait_ge(s_sp.h, s_sp.n)
            SY.wait_ge(s_v.h, P4_V)
            GUL = lambda k: glb[:, (k % 2) * WQ:(k % 2) * WQ + WQ]
            UQ5 = tmpC1[:, 2 * WQ:3 * WQ]
            UD1 = tmpC1[:, 3 * WQ:4 * WQ]
            UD2 = tmpC1[:, 2 * WQ:3 * WQ]
            OAC = lambda db: tmpC1[:, (4 + db % 2) * WQ:(4 + db % 2) * WQ + WQ]
            XRE = lambda db: tmpC1[:, (6 + db % 2) * WQ:(6 + db % 2) * WQ + WQ]
            dn_pe_prev = []
            for tq in range(TQ):
                gq0 = s_ld.n
                vq = []
                for k in range(KI):
                    if k >= 2:
                        SY.wait_ge(s_v.h, vq[k - 2])
                    s_ld.dma(SY.dma_start(GUL(k), gu_sp[k * 128:(k + 1) * 128, tq * WQ:(tq + 1) * WQ]))
                    V.wait_ge(s_ld.h, gq0 + 16 * (k + 1))
                    if k == 0:
                        V.wait_ge(s_row.h, ROW_INV2)
                        if tq > 0:
                            V.wait_ge(s_pe.h, dn_pe_prev[-1])
                        else:
                            V.wait_ge(s_pe.h, s_pe.n)
                    V.tensor_mul(UQ5, GUL(k), bc0[:, tq * WQ:(tq + 1) * WQ])
                    s_v.inc(V.tensor_scalar(q2q(k), UQ5, CC, CC, OP.add, OP.subtract))
                    vq.append(s_v.n)
                Q2_V = s_v.n
                wd5 = s_ld.n
                dn_pe = []
                dn_v = []
                for db in range(NB):
                    wslot = wdqB[:, (db % 2) * KI * 128:((db % 2) + 1) * KI * 128]
                    if db >= 2:
                        SY.wait_ge(s_pe.h, dn_pe[db - 2])
                        SY.wait_ge(s_v.h, dn_v[db - 2])
                    s_ld.dma(SY.dma_start(
                        wslot.rearrange("p (k c) -> p k c", c=128),
                        wdq_sp.rearrange("(k p) (o c) -> p k o c", p=128, c=128)[:, :, db, :]))
                    s_ld.dma(SY.dma_start(OAC(db), oacc_sp[db * 128:(db + 1) * 128,
                                                           tq * WQ:(tq + 1) * WQ]))
                    s_ld.dma(SY.dma_start(XRE(db), xT[db * 128:(db + 1) * 128,
                                                      tq * WQ:(tq + 1) * WQ]))
                    PE.wait_ge(s_ld.h, wd5 + 48 * (db + 1))
                    if db == 0:
                        PE.wait_ge(s_v.h, Q2_V)
                        if tq == 0:
                            PE.wait_ge(s_sp.h, WDQ_SP)
                    if db >= 2:
                        PE.wait_ge(s_v.h, dn_v[db - 2])
                    wv = wslot.rearrange("p (k c) -> p k c", c=128)
                    for k in range(KI):
                        mm = PE.matmul(BANK(db % 2), wv[:, k, :], q2q(k),
                                       start=(k == 0), stop=(k == KI - 1))
                    s_pe.inc(mm)
                    dn_pe.append(s_pe.n)
                    V.wait_ge(s_pe.h, dn_pe[db])
                    V.wait_ge(s_fin.h, s_fin.n)
                    V.tensor_mul(UD1, BANK(db % 2), bc1[:, tq * WQ:(tq + 1) * WQ])
                    V.scalar_tensor_tensor(UD2, UD1, 1.0, OAC(db), OP.mult, OP.add)
                    s_v.inc(V.scalar_tensor_tensor(UD1, UD2, 1.0, XRE(db), OP.mult, OP.add))
                    dn_v.append(s_v.n)
                    SY.wait_ge(s_v.h, dn_v[db])
                    s_fin.dma(SY.dma_start(outT[db * 128:(db + 1) * 128, tq * WQ:(tq + 1) * WQ],
                                           UD1))
                dn_pe_prev = dn_pe
            SY.wait_ge(s_fin.h, s_fin.n)
    return nc


def kernel(L0_attn_x, L0_attn_Wr, L0_attn_Wf, L0_attn_Wc, L0_attn_Wo,
           L0_attn_ln_scale, L0_ffn_w_gate, L0_ffn_w_up, L0_ffn_w_down):
    x = np.ascontiguousarray(np.asarray(L0_attn_x, dtype=np.float32))
    B = x.shape[0]
    shared = dict(
        Wr=np.ascontiguousarray(np.asarray(L0_attn_Wr, dtype=np.float32)),
        Wf=np.ascontiguousarray(np.asarray(L0_attn_Wf, dtype=np.float32)),
        Wc=np.ascontiguousarray(np.asarray(L0_attn_Wc, dtype=np.float32)),
        Wo=np.ascontiguousarray(np.asarray(L0_attn_Wo, dtype=np.float32)),
        lns=np.ascontiguousarray(np.asarray(L0_attn_ln_scale, dtype=np.float32).reshape(16, 128).T),
        wgT=np.ascontiguousarray(np.asarray(L0_ffn_w_gate, dtype=np.float32).T),
        wuT=np.ascontiguousarray(np.asarray(L0_ffn_w_up, dtype=np.float32).T),
        wdT=np.ascontiguousarray(np.asarray(L0_ffn_w_down, dtype=np.float32).T),
    )
    in_maps = [dict(shared, xT=np.ascontiguousarray(x[b].T)) for b in range(B)]
    nc = build()
    res = run_bass_kernel_spmd(nc, in_maps, core_ids=list(range(B)))
    out = np.stack([np.asarray(r["outT"], dtype=np.float32).T for r in res.results], axis=0)
    return np.ascontiguousarray(out)


# version 14 (final)
# speedup vs baseline: 1.1411x; 1.0167x over previous
from contextlib import ExitStack
import numpy as np
import ml_dtypes
import concourse.bass as bass
import concourse.mybir as mybir
from concourse.bass_utils import run_bass_kernel_spmd

F32 = mybir.dt.float32
BF16 = mybir.dt.bfloat16
FP16 = mybir.dt.float16
I32 = mybir.dt.int32
AF = mybir.ActivationFunctionType
OP = mybir.AluOpType
AX = mybir.AxisListType

D = 2048
T = 2048
HID = 8192
NB = D // 128
KI = HID // 128
W = 512
NW = T // W
TQ = 4
WQ = T // TQ
EPS = 1e-5
C32 = float(np.float32(1.5 * 2 ** 23))
C52 = float(np.float32(1.5 * 2 ** 52))


class Sem:
    def __init__(self, nc, st, name):
        self.h = st.enter_context(nc.semaphore(name))
        self.n = 0

    def inc(self, instr, k=1):
        instr.then_inc(self.h, k)
        self.n += k
        return instr

    def dma(self, instr):
        return self.inc(instr, 16)


def build(dbg=False, dbgwin=None):
    if dbgwin is None:
        dbgwin = dbg
    nc = bass.Bass("TRN2", target_bir_lowering=False, detect_race_conditions=False)

    xT = nc.dram_tensor("xT", [D, T], F32, kind="ExternalInput").ap()
    Wr = nc.dram_tensor("Wr", [D, D], F32, kind="ExternalInput").ap()
    Wf = nc.dram_tensor("Wf", [D, D], F32, kind="ExternalInput").ap()
    Wc = nc.dram_tensor("Wc", [D, D], F32, kind="ExternalInput").ap()
    Wo = nc.dram_tensor("Wo", [D, D], F32, kind="ExternalInput").ap()
    lns = nc.dram_tensor("lns", [128, NB], F32, kind="ExternalInput").ap()
    wgT = nc.dram_tensor("wgT", [D, HID], F32, kind="ExternalInput").ap()
    wuT = nc.dram_tensor("wuT", [D, HID], F32, kind="ExternalInput").ap()
    wdT = nc.dram_tensor("wdT", [HID, D], F32, kind="ExternalInput").ap()
    outT = nc.dram_tensor("outT", [D, T], F32, kind="ExternalOutput").ap()

    KIND = "ExternalOutput" if dbg else "Internal"
    if dbg:
        f_dbg = nc.dram_tensor("f_dbg", [D, T], F32, kind="ExternalOutput").ap()
        c_dbg = nc.dram_tensor("c_dbg", [D, T], F32, kind="ExternalOutput").ap()
        h_dbg = nc.dram_tensor("h_dbg", [D, T], F32, kind="ExternalOutput").ap()
        r_dbg = nc.dram_tensor("r_dbg", [D, T], F32, kind="ExternalOutput").ap()
        g_dbg = nc.dram_tensor("g_dbg", [D, T], BF16, kind="ExternalOutput").ap()
    xq_sp = nc.dram_tensor("xq_sp", [D, T], BF16, kind=KIND).ap()
    gu_sp = nc.dram_tensor("gu_sp", [HID, T], F32, kind=KIND).ap()
    wdq_sp = nc.dram_tensor("wdq_sp", [HID, D], BF16, kind=KIND).ap()
    oacc_sp = nc.dram_tensor("oacc_sp", [D, T], F32, kind=KIND).ap()
    rows_sp = nc.dram_tensor("rows_sp", [8, T], F32, kind=KIND).ap()

    st = ExitStack()
    sbt = lambda n_, sh, dt: st.enter_context(nc.sbuf_tensor(n_, sh, dt))
    with st:
        actA = sbt("actA", [128, NB * T], BF16)
        tmpC1 = sbt("tmpC1", [128, 2 * T], F32)
        rows = sbt("rows", [1, 2 * T], F32)
        bc0 = sbt("bc0", [128, T], F32)
        bc1 = sbt("bc1", [128, T], F32)
        cols = sbt("cols", [128, 64], F32)
        d_i32 = sbt("d_i32", [128, 1], I32)
        scaf = sbt("scaf", [128, 32], F32)
        psum = st.enter_context(nc.psum_tensor("ps", [128, 4096], F32))
        BANK = lambda b: psum[:, b * 512:(b + 1) * 512]
        PHALF = lambda h: psum[:, h * 2048:h * 2048 + 2048]

        CC = cols[:, 0:1]
        LNSC = lambda b: cols[:, 1 + b:2 + b]
        CAR = lambda b: cols[:, 17 + b:18 + b]
        INV127 = cols[:, 33:34]
        R0 = rows[:, 0:T]
        R1 = rows[:, T:2 * T]

        s_ld = Sem(nc, st, "s_ld")
        s_wld = Sem(nc, st, "s_wld")
        s_g = Sem(nc, st, "s_g")
        s_v = Sem(nc, st, "s_v")
        s_s = Sem(nc, st, "s_s")
        s_pe = Sem(nc, st, "s_pe")
        s_cast = Sem(nc, st, "s_cast")
        s_row = Sem(nc, st, "s_row")
        s_sp = Sem(nc, st, "s_sp")
        s_fin = Sem(nc, st, "s_fin")

        V, S, G, PE, SY = nc.vector, nc.scalar, nc.gpsimd, nc.tensor, nc.sync
        SQ = nc.scalar

        def part_max_tree(a, b, c, psum_src=None):
            if psum_src is not None:
                s_v.inc(V.tensor_copy(a, psum_src))
            tri = [a, b, c]
            cur = 0
            for n in (64, 32, 16, 8, 4, 2, 1):
                aa, bb, cc = tri[cur % 3], tri[(cur + 1) % 3], tri[(cur + 2) % 3]
                s_v.inc(V.nop()) if False else None
                SY.wait_ge(s_v.h, s_v.n)
                s_row.dma(SY.dma_start(bb[0:n, :], aa[n:2 * n, :]))
                V.wait_ge(s_row.h, s_row.n)
                s_v.inc(V.scalar_tensor_tensor(cc[0:n, :], bb[0:n, :], 1.0, aa[0:n, :],
                                               OP.mult, OP.max))
                cur += 2
            return tri[cur % 3]

        s_v.inc(V.memset(scaf[:, 0:1], 0.7))
        V.wait_ge(s_v.h, s_v.n)
        s_v.inc(V.tensor_scalar(d_i32[:, :], scaf[:, 0:1], 0.0, None, OP.add))
        G.wait_ge(s_v.h, s_v.n)
        s_cast.dma(G.dma_start(scaf[:, 1:2], d_i32[:, :]))
        V.wait_ge(s_cast.h, s_cast.n)
        V.tensor_scalar(scaf[:, 2:3], scaf[:, 1:2], C32, None, OP.mult)
        V.tensor_scalar(scaf[:, 3:4], scaf[:, 1:2], -C52, C52, OP.mult, OP.add)
        V.tensor_add(CC, scaf[:, 2:3], scaf[:, 3:4])
        V.tensor_scalar(cols[:, 34:35], scaf[:, 1:2], -3e-7, 1.0, OP.mult, OP.add)
        s_v.inc(V.memset(scaf[:, 14:15], 1.0))
        ONES = scaf[:, 14:15]
        s_v.inc(V.memset(scaf[:, 15:16], 127.0))
        V.wait_ge(s_v.h, s_v.n)
        s_v.inc(V.reciprocal(INV127, scaf[:, 15:16]))
        V_CONST = s_v.n
        s_ld.dma(SY.dma_start(cols[:, 1:17], lns))
        LNS_LD = s_ld.n

        stP12 = ExitStack()
        gld = stP12.enter_context(nc.sbuf_tensor("gld", [128, 2 * 2048], F32))
        junk2 = stP12.enter_context(nc.sbuf_tensor("junk2", [128, 2048], F32))
        tmpC2 = stP12.enter_context(nc.sbuf_tensor("tmpC2", [128, 2 * T], F32))
        xstg = stP12.enter_context(nc.sbuf_tensor("xstg", [128, T], BF16))
        with stP12:
            p1chunks = []
            for mat, nrow, wid, slot in ((wgT, NB, HID, 4), (wuT, NB, HID, 5), (wdT, KI, D, 6)):
                for rchunk in range(nrow):
                    for c0 in range(0, wid, 2048):
                        p1chunks.append((mat, rchunk, c0, 2048, slot))
            accslot = {4: (4, 20), 5: (5, 21), 6: (6, 22)}
            nacc = {4: 0, 5: 0, 6: 0}
            gdone = []
            sdone = []
            for ci, (mat, rc, c0, wid, slot) in enumerate(p1chunks):
                buf = gld[:, (ci % 2) * 2048:(ci % 2) * 2048 + wid]
                q, qs = (SQ, s_wld) if ci % 2 == 0 else (SY, s_ld)
                if ci >= 2:
                    q.wait_ge(s_g.h, gdone[ci - 2])
                qs.dma(q.dma_start(buf, mat[rc * 128:(rc + 1) * 128, c0:c0 + wid]))
                S.wait_ge(qs.h, qs.n)
                par = scaf[:, 8 + ci % 2:9 + ci % 2]
                if ci >= 2:
                    S.wait_ge(s_g.h, gdone[ci - 1])
                s_s.inc(S.activation(junk2[:, 0:wid], buf, AF.Abs, accum_out=par))
                sdone.append(s_s.n)
                G.wait_ge(s_s.h, sdone[ci])
                a, b = accslot[slot]
                k = nacc[slot]
                if k == 0:
                    s_g.inc(G.tensor_copy(scaf[:, a:a + 1], par))
                else:
                    psrc = scaf[:, (a if k % 2 == 1 else b):(a if k % 2 == 1 else b) + 1]
                    pdst = scaf[:, (b if k % 2 == 1 else a):(b if k % 2 == 1 else a) + 1]
                    s_g.inc(G.tensor_tensor(pdst, par, psrc, OP.add))
                nacc[slot] += 1
                gdone.append(s_g.n)
            for slot in (4, 5, 6):
                a, b = accslot[slot]
                fin = a if (nacc[slot] % 2 == 1) else b
                if fin != slot:
                    s_g.inc(G.tensor_copy(scaf[:, slot:slot + 1], scaf[:, fin:fin + 1]))
            G_PARTIALS = s_g.n
            SCg, SCu, SCd = scaf[:, 16:17], scaf[:, 17:18], scaf[:, 18:19]
            INVg, INVu, INVd = scaf[:, 10:11], scaf[:, 11:12], scaf[:, 12:13]

            xblk = lambda b: xT[b * 128:(b + 1) * 128, :]
            XA = lambda b: tmpC1[:, (b % 2) * T:(b % 2) * T + T]
            XSQ = lambda b: tmpC2[:, (b % 2) * T:(b % 2) * T + T]
            pa0 = s_ld.n
            pe_blk = []
            for b in range(NB):
                if b >= 2:
                    SY.wait_ge(s_pe.h, pe_blk[b - 2])
                s_ld.dma(SY.dma_start(XA(b), xblk(b)))
                PE.wait_ge(s_ld.h, pa0 + 16 * (b + 1))
                if b == 0:
                    PE.wait_ge(s_v.h, V_CONST)
                for w in range(NW):
                    mm = PE.matmul(psum[0:1, w * W:(w + 1) * W], ONES, XA(b)[:, w * W:(w + 1) * W],
                                   start=(b == 0), stop=(b == NB - 1))
                s_pe.inc(mm)
                pe_blk.append(s_pe.n)
            ra = tmpC2[0:1, 0:T]
            rb = tmpC2[0:1, T:2 * T]
            V.wait_ge(s_pe.h, s_pe.n)
            V.tensor_scalar(R0, psum[0:1, 0:2048], 1.0 / D, None, OP.mult)
            s_v.inc(V.tensor_scalar(R1, R0, -1.0, None, OP.mult))
            SY.wait_ge(s_v.h, s_v.n)
            s_row.dma(SY.dma_start(rows_sp[1:2, :], R1))
            SY.wait_ge(s_row.h, s_row.n)
            s_row.dma(SY.dma_start(bc0[:, :], rows_sp[1:2, :].broadcast_to([128, T])))
            ROW_NMU = s_row.n
            pa1 = s_ld.n
            pe_blk2 = []
            v_blk0 = []
            ss2 = []
            for b in range(NB):
                if b >= 2:
                    SY.wait_ge(s_pe.h, pe_blk2[b - 2])
                s_ld.dma(SY.dma_start(XA(b), xblk(b)))
                V.wait_ge(s_ld.h, pa1 + 16 * (b + 1))
                if b == 0:
                    V.wait_ge(s_row.h, ROW_NMU)
                if b >= 2:
                    V.wait_ge(s_s.h, ss2[b - 2])
                s_v.inc(V.tensor_add(XSQ(b), XA(b), bc0[:, :]))
                v_blk0.append(s_v.n)
                S.wait_ge(s_v.h, v_blk0[b])
                if b >= 2:
                    S.wait_ge(s_pe.h, pe_blk2[b - 2])
                s_s.inc(S.activation(XA(b), XSQ(b), AF.Square))
                ss2.append(s_s.n)
                PE.wait_ge(s_s.h, s_s.n)
                for w in range(NW):
                    mm = PE.matmul(psum[0:1, 2048 + w * W:2048 + (w + 1) * W], ONES,
                                   XA(b)[:, w * W:(w + 1) * W], start=(b == 0), stop=(b == NB - 1))
                s_pe.inc(mm)
                pe_blk2.append(s_pe.n)
            V.wait_ge(s_pe.h, s_pe.n)
            s_v.inc(V.tensor_scalar(rb, psum[0:1, 2048:4096], 1.0 / D, EPS, OP.mult, OP.add))
            V_STATS = s_v.n
            S.wait_ge(s_v.h, s_v.n)
            s_s.inc(S.activation(ra, rb, AF.Sqrt))
            V.wait_ge(s_s.h, s_s.n)
            V.reciprocal(R0, ra)
            V.tensor_mul(ra, R0, R0)
            V.tensor_mul(ra, ra, rb)
            V.tensor_scalar(ra, ra, -0.5, 1.5, OP.mult, OP.add)
            V.tensor_mul(rb, R0, ra)
            V.tensor_mul(ra, R1, rb)
            V.tensor_copy(R1, rb)
            s_v.inc(V.tensor_copy(R0, ra))
            SY.wait_ge(s_v.h, s_v.n)
            s_row.dma(SY.dma_start(rows_sp[0:1, :], R1))
            s_row.dma(SY.dma_start(rows_sp[1:2, :], R0))
            SY.wait_ge(s_row.h, s_row.n)
            s_row.dma(SY.dma_start(bc0[:, :], rows_sp[0:1, :].broadcast_to([128, T])))
            s_row.dma(SY.dma_start(bc1[:, :], rows_sp[1:2, :].broadcast_to([128, T])))
            ROW_RS = s_row.n
            xnT = lambda b: actA[:, b * T:(b + 1) * T]
            UB1 = lambda b: tmpC2[:, (b % 2) * T:(b % 2) * T + T]
            s_v.inc(V.memset(PHALF(1), 0.0))
            V.wait_ge(s_v.h, s_v.n)
            pb0 = s_ld.n
            v_blk = []
            for b in range(NB):
                if b >= 2:
                    SY.wait_ge(s_v.h, v_blk[b - 2])
                s_ld.dma(SY.dma_start(XA(b), xblk(b)))
                V.wait_ge(s_ld.h, pb0 + 16 * (b + 1))
                if b == 0:
                    V.wait_ge(s_row.h, ROW_RS)
                u = UB1(b)
                V.tensor_mul(u, XA(b), bc0[:, :])
                V.scalar_tensor_tensor(xnT(b), u, 1.0, bc1[:, :], OP.mult, OP.add)
                V.scalar_tensor_tensor(XA(b), u, 1.0, bc1[:, :], OP.mult, OP.add)
                V.scalar_tensor_tensor(u, XA(b), -1.0, XA(b), OP.mult, OP.max)
                s_v.inc(V.scalar_tensor_tensor(PHALF(b % 2), u, 1.0, PHALF(1 - b % 2),
                                               OP.mult, OP.max))
                v_blk.append(s_v.n)
            TRA = tmpC2[:, 0:T]
            TRB = tmpC2[:, T:2 * T]
            TRC = bc0[:, :]
            TOPROW = part_max_tree(TRA, TRB, TRC, psum_src=PHALF(1))
            V_PSUMFREE = s_v.n
            V.tensor_scalar(R0, TOPROW[0:1, :], INV127[0:1, :], EPS, OP.mult, OP.add)
            s_v.inc(V.reciprocal(R1, R0))
            SY.wait_ge(s_v.h, s_v.n)
            s_row.dma(SY.dma_start(rows_sp[2:3, :], R0))
            SY.wait_ge(s_row.h, s_row.n)
            s_row.dma(SY.dma_start(tmpC2[0:1, 0:T], rows_sp[0:1, :]))
            s_row.dma(SY.dma_start(tmpC2[0:1, T:2 * T], rows_sp[1:2, :]))
            V.wait_ge(s_row.h, s_row.n)
            V.tensor_mul(R0, ra, R1)
            V.tensor_mul(ra, rb, R1)
            s_v.inc(V.tensor_copy(R1, ra))
            SY.wait_ge(s_v.h, s_v.n)
            s_row.dma(SY.dma_start(rows_sp[4:5, :], R0))
            s_row.dma(SY.dma_start(rows_sp[5:6, :], R1))
            SY.wait_ge(s_row.h, s_row.n)
            s_row.dma(SY.dma_start(bc0[:, :], rows_sp[4:5, :].broadcast_to([128, T])))
            s_row.dma(SY.dma_start(bc1[:, :], rows_sp[5:6, :].broadcast_to([128, T])))
            ROW_RQ = s_row.n
            pc0 = s_ld.n
            v_blk2 = []
            sp_blk2 = []
            for b in range(NB):
                if b >= 2:
                    SY.wait_ge(s_v.h, v_blk2[b - 2])
                s_ld.dma(SY.dma_start(XA(b), xblk(b)))
                V.wait_ge(s_ld.h, pc0 + 16 * (b + 1))
                if b == 0:
                    V.wait_ge(s_row.h, ROW_RQ)
                    V.wait_ge(s_v.h, V_CONST)
                if b >= 1:
                    V.wait_ge(s_sp.h, sp_blk2[b - 1])
                u = UB1(b)
                V.tensor_mul(u, XA(b), bc0[:, :])
                V.scalar_tensor_tensor(XA(b), u, 1.0, bc1[:, :], OP.mult, OP.add)
                s_v.inc(V.tensor_scalar(xstg[:, :], XA(b), CC, CC, OP.add, OP.subtract))
                v_blk2.append(s_v.n)
                SY.wait_ge(s_v.h, v_blk2[b])
                s_sp.dma(SY.dma_start(xq_sp[b * 128:(b + 1) * 128, :], xstg[:, :]))
                sp_blk2.append(s_sp.n)
            XQ_SPILLED = s_sp.n
        stP3 = ExitStack()
        actB = stP3.enter_context(nc.sbuf_tensor("actB", [128, NB * T], BF16))
        wbuf = stP3.enter_context(nc.sbuf_tensor("wbuf", [128, 6 * NB * 128], BF16))
        heven = stP3.enter_context(nc.sbuf_tensor("heven", [128, T // 2], F32))
        pwin = stP3.enter_context(nc.sbuf_tensor("pwin", [128, W], BF16))
        with stP3:
            WSL = lambda s: wbuf[:, s * NB * 128:(s + 1) * NB * 128].rearrange("p (k c) -> p k c", c=128)
            castsrc = lambda mat, ob: mat.rearrange("(k p) (o c) -> p k o c", p=128, c=128)[:, :, ob, :]
            Gblk = lambda b: actB[:, b * T:(b + 1) * T]
            fbF = tmpC1[:, 0:T]
            cbF = tmpC1[:, T:2 * T]
            _pyr = {}
            off0 = 0
            for k in range(1, 12):
                n = T >> k
                _pyr[k] = n
            pyr_f = {1: bc0[:, 0:1024], 2: bc1[:, 0:512]}
            pyr_c = {1: bc0[:, 1024:2048], 2: bc1[:, 512:1024]}
            off = 1024
            for k in range(3, 12):
                n = T >> k
                pyr_f[k] = bc1[:, off:off + n]; off += n
                pyr_c[k] = bc1[:, off:off + n]; off += n
            lev_f = {0: fbF}
            lev_c = {0: cbF}
            lev_f.update(pyr_f)
            lev_c.update(pyr_c)
            PS_T3 = lambda n: psum[:, 0:n]
            PS_T4 = lambda n: psum[:, 2048:2048 + n]

            def vbar(ins):
                s_v.inc(ins)
                V.wait_ge(s_v.h, s_v.n)
                return ins
            V.wait_ge(s_ld.h, LNS_LD)
            G.wait_ge(s_wld.h, s_wld.n)
            G.wait_ge(s_ld.h, s_ld.n)
            G.wait_ge(s_sp.h, XQ_SPILLED)
            G.wait_ge(s_v.h, s_v.n)
            G.wait_ge(s_row.h, s_row.n)
            cast_ob = []
            for ob in range(2):
                for mi, mat in enumerate((Wf, Wc, Wr)):
                    s_cast.dma(G.dma_start(WSL(3 * (ob % 2) + mi), castsrc(mat, ob)))
                cast_ob.append(s_cast.n)
            prev_s = s_s.n
            prev_v = s_v.n
            rowacc_v = []
            for ob in range(NB):
                sl = 3 * (ob % 2)
                PE.wait_ge(s_cast.h, cast_ob[ob])
                PE.wait_ge(s_s.h, prev_s)
                PE.wait_ge(s_v.h, prev_v)
                if ob == 0:
                    PE.wait_ge(s_v.h, V_PSUMFREE)
                for tw in range(NW):
                    for k in range(NB):
                        PE.matmul(BANK(tw), WSL(sl + 0)[:, k, :], xnT(k)[:, tw * W:(tw + 1) * W],
                                  start=(k == 0), stop=(k == NB - 1))
                for tw in range(NW):
                    for k in range(NB):
                        mm = PE.matmul(BANK(4 + tw), WSL(sl + 1)[:, k, :], xnT(k)[:, tw * W:(tw + 1) * W],
                                       start=(k == 0), stop=(k == NB - 1))
                s_pe.inc(mm)
                PE_FC = s_pe.n
                S.wait_ge(s_pe.h, PE_FC)
                for tw in range(NW):
                    S.activation(fbF[:, tw * W:(tw + 1) * W], BANK(tw), AF.Sigmoid)
                for tw in range(NW):
                    ins_ = S.activation(cbF[:, tw * W:(tw + 1) * W], BANK(4 + tw), AF.Silu)
                s_s.inc(ins_)
                S_FC = s_s.n
                if dbgwin:
                    SY.wait_ge(s_s.h, S_FC)
                    s_sp.dma(SY.dma_start(f_dbg[ob * 128:(ob + 1) * 128, :], fbF))
                    s_sp.dma(SY.dma_start(c_dbg[ob * 128:(ob + 1) * 128, :], cbF))
                    SY_DBG_FC = s_sp.n
                V.wait_ge(s_s.h, S_FC)
                for k in range(0, 11):
                    n = T >> (k + 1)
                    sf, sc = lev_f[k], lev_c[k]
                    df, dc = lev_f[k + 1], lev_c[k + 1]
                    fa = sf[:, 0:2 * n:2]
                    fb = sf[:, 1:2 * n:2]
                    ca = sc[:, 0:2 * n:2]
                    cb = sc[:, 1:2 * n:2]
                    vbar(V.tensor_mul(df, fa, fb))
                    vbar(V.tensor_sub(PS_T3(n), ca, cb))
                    vbar(V.tensor_mul(heven[:, 0:n], PS_T3(n), fb))
                    vbar(V.tensor_add(dc, heven[:, 0:n], cb))
                for k in range(10, 0, -1):
                    n = T >> k
                    half = n // 2
                    of = lev_f[k + 1]
                    oc = lev_c[k + 1]
                    ef_src = lev_f[k]
                    ec_src = lev_c[k]
                    m = half - 1
                    if m > 0:
                        xf = ef_src[:, 2:n:2]
                        xc = ec_src[:, 2:n:2]
                        vbar(V.tensor_mul(PS_T4(m), of[:, 0:m], xf))
                        vbar(V.tensor_sub(PS_T3(m), oc[:, 0:m], xc))
                        vbar(V.tensor_mul(heven[:, 0:m], PS_T3(m), xf))
                        vbar(V.tensor_add(psum[:, 1024:1024 + m], heven[:, 0:m], xc))
                        vbar(V.tensor_copy(ec_src[:, 2:n:2], psum[:, 1024:1024 + m]))
                        vbar(V.tensor_copy(ef_src[:, 2:n:2], PS_T4(m)))
                    vbar(V.tensor_copy(ec_src[:, 1:n:2], oc[:, 0:half]))
                    vbar(V.tensor_copy(ef_src[:, 1:n:2], of[:, 0:half]))
                m = T // 2 - 1
                xf0 = fbF[:, 2:T:2]
                xc0 = cbF[:, 2:T:2]
                vbar(V.tensor_sub(PS_T3(m), pyr_c[1][:, 0:m], xc0))
                vbar(V.tensor_mul(psum[:, 1024:1024 + m], PS_T3(m), xf0))
                vbar(V.tensor_add(heven[:, 1:T // 2], psum[:, 1024:1024 + m], xc0))
                vbar(V.tensor_copy(heven[:, 0:1], cbF[:, 0:1]))
                if dbgwin:
                    SY.wait_ge(s_v.h, s_v.n + 0) if False else None
                if dbgwin:
                    s_v.inc(V.nop()) if False else None
                    _hm = s_v.n
                    s_v.inc(V.tensor_copy(pwin[:, 0:1], heven[:, 0:1]))
                    SY.wait_ge(s_v.h, s_v.n)
                    s_sp.dma(SY.dma_start(h_dbg[ob * 128:(ob + 1) * 128, 0:T // 2], heven[:, :]))
                    s_sp.dma(SY.dma_start(h_dbg[ob * 128:(ob + 1) * 128, T // 2:T], pyr_c[1]))
                    SY.wait_ge(s_sp.h, s_sp.n)
                vbar(V.tensor_mul(fbF[:, 0:T:2], heven[:, :], heven[:, :]))
                vbar(V.tensor_mul(fbF[:, 1:T:2], pyr_c[1], pyr_c[1]))
                V_H2 = s_v.n
                PE.wait_ge(s_v.h, V_H2)
                for tw in range(NW):
                    mm = PE.matmul(psum[0:1, tw * W:(tw + 1) * W], ONES, fbF[:, tw * W:(tw + 1) * W],
                                   start=True, stop=True)
                s_pe.inc(mm)
                V.wait_ge(s_pe.h, s_pe.n)
                if ob == 0:
                    s_v.inc(V.tensor_copy(R0, psum[0:1, 0:2048]))
                else:
                    dst = R0 if ob % 2 == 0 else R1
                    srcr = R1 if ob % 2 == 0 else R0
                    s_v.inc(V.tensor_add(dst, psum[0:1, 0:2048], srcr))
                rowacc_v.append(s_v.n)
                PE.wait_ge(s_v.h, rowacc_v[-1])
                for tw in range(NW):
                    for k in range(NB):
                        mm = PE.matmul(BANK(tw), WSL(sl + 2)[:, k, :], xnT(k)[:, tw * W:(tw + 1) * W],
                                       start=(k == 0), stop=(k == NB - 1))
                s_pe.inc(mm)
                PE_R = s_pe.n
                if ob + 2 < NB:
                    G.wait_ge(s_pe.h, PE_R)
                    for mi, mat in enumerate((Wf, Wc, Wr)):
                        s_cast.dma(G.dma_start(WSL(3 * (ob % 2) + mi), castsrc(mat, ob + 2)))
                    cast_ob.append(s_cast.n)
                else:
                    cast_ob.append(s_cast.n)
                S.wait_ge(s_pe.h, PE_R)
                for tw in range(NW):
                    ins_ = S.activation(BANK(4 + tw), BANK(tw), AF.Silu)
                s_s.inc(ins_)
                prev_s = s_s.n
                V.wait_ge(s_s.h, prev_s)
                for tw in range(NW):
                    rw = BANK(4 + tw)
                    V.tensor_mul(pwin[:, 0:W:2], rw[:, 0:W:2], heven[:, tw * 256:(tw + 1) * 256])
                    vbar(V.tensor_mul(pwin[:, 1:W:2], rw[:, 1:W:2], pyr_c[1][:, tw * 256:(tw + 1) * 256]))
                    s_v.inc(V.tensor_scalar(Gblk(ob)[:, tw * W:(tw + 1) * W], pwin[:, :],
                                            LNSC(ob), None, OP.mult))
                prev_v = s_v.n
                if dbgwin:
                    SY.wait_ge(s_v.h, prev_v)
                    s_sp.dma(SY.dma_start(g_dbg[ob * 128:(ob + 1) * 128, :], Gblk(ob)))
                    s_sp.dma(SY.dma_start(r_dbg[ob * 128:(ob + 1) * 128, :], fbF))
                    SY.wait_ge(s_sp.h, s_sp.n)
            ra3 = tmpC1[0:1, 0:T]
            rb3 = tmpC1[0:1, T:2 * T]
            MSROW = R1
            s_v.inc(V.tensor_scalar(rb3, MSROW, 1.0 / D, EPS, OP.mult, OP.add))
            V_MSROW = s_v.n
            S.wait_ge(s_v.h, V_MSROW)
            s_s.inc(S.activation(ra3, rb3, AF.Sqrt))
            V.wait_ge(s_s.h, s_s.n)
            V.reciprocal(R0, ra3)
            V.tensor_mul(ra3, R0, R0)
            V.tensor_mul(ra3, ra3, rb3)
            V.tensor_scalar(ra3, ra3, -0.5, 1.5, OP.mult, OP.add)
            s_v.inc(V.tensor_mul(R1, R0, ra3))
            SY.wait_ge(s_v.h, s_v.n)
            s_row.dma(SY.dma_start(rows_sp[6:7, :], R1))
            SY.wait_ge(s_row.h, s_row.n)
            SY.wait_ge(s_v.h, s_v.n)
            s_row.dma(SY.dma_start(bc0[:, :], rows_sp[6:7, :].broadcast_to([128, T])))
            ROW_RSH = s_row.n
            OUTB = tmpC1[:, T:2 * T]
            V_GP = s_v.n
            oy_v = []
            oy_sp = []
            for ob in range(NB):
                G.wait_ge(s_pe.h, s_pe.n)
                s_cast.dma(G.dma_start(WSL(ob % 2), castsrc(Wo, ob)))
                PE.wait_ge(s_cast.h, s_cast.n)
                if ob == 0:
                    PE.wait_ge(s_v.h, V_GP)
                if ob >= 1:
                    PE.wait_ge(s_v.h, oy_v[ob - 1])
                for tw in range(NW):
                    if tw >= 1:
                        PE.wait_ge(s_v.h, s_v.n)
                    for k in range(NB):
                        mm = PE.matmul(BANK(7), WSL(ob % 2)[:, k, :],
                                       Gblk(k)[:, tw * W:(tw + 1) * W],
                                       start=(k == 0), stop=(k == NB - 1))
                    s_pe.inc(mm)
                    V.wait_ge(s_pe.h, s_pe.n)
                    if ob == 0 and tw == 0:
                        V.wait_ge(s_row.h, ROW_RSH)
                    if ob >= 1 and tw == 0:
                        V.wait_ge(s_sp.h, oy_sp[ob - 1])
                    s_v.inc(V.tensor_mul(OUTB[:, tw * W:(tw + 1) * W], BANK(7),
                                         bc0[:, tw * W:(tw + 1) * W]))
                oy_v.append(s_v.n)
                SY.wait_ge(s_v.h, oy_v[ob])
                s_sp.dma(SY.dma_start(oacc_sp[ob * 128:(ob + 1) * 128, :], OUTB))
                oy_sp.append(s_sp.n)
            OACC_SP = s_sp.n
            PE.wait_ge(s_g.h, G_PARTIALS)
            for i, slot in enumerate((4, 5, 6)):
                mm = PE.matmul(psum[0:1, 1536 + i:1537 + i], scaf[:, slot:slot + 1], ONES,
                               start=True, stop=True)
            s_pe.inc(mm)
            V.wait_ge(s_pe.h, s_pe.n)
            V.tensor_scalar(scaf[0:1, 20:23], psum[0:1, 1536:1539],
                            1.0 / (D * HID), EPS, OP.mult, OP.add)
            s_v.inc(V.tensor_scalar(scaf[0:1, 16:19], scaf[0:1, 20:23],
                                    cols[0:1, 34:35], None, OP.mult))
            SY.wait_ge(s_v.h, s_v.n)
            s_row.dma(SY.dma_start(rows_sp[7:8, 0:3], scaf[0:1, 16:19]))
            s_row.dma(SY.dma_start(rows_sp[7:8, 3:6], scaf[0:1, 0:3]))
            s_row.dma(SY.dma_start(rows_sp[7:8, 6:7], cols[0:1, 0:1]))
            SY.wait_ge(s_row.h, s_row.n)
            s_row.dma(SY.dma_start(scaf[:, 16:19], rows_sp[7:8, 0:3].broadcast_to([128, 3])))
            ROW_SCALES = s_row.n
            V.wait_ge(s_row.h, ROW_SCALES)
            s_v.inc(V.reciprocal(scaf[:, 10:13], scaf[:, 16:19]))
            V_INVS = s_v.n
            P3_PE = s_pe.n
            P3_V = s_v.n
        stP4 = ExitStack()
        sgb = stP4.enter_context(nc.sbuf_tensor("sgb", [128, 1024], F32))
        wldF = stP4.enter_context(nc.sbuf_tensor("wldF", [128, 2 * D], F32))
        wqB = stP4.enter_context(nc.sbuf_tensor("wqB", [128, 4 * D], BF16))
        mxb = stP4.enter_context(nc.sbuf_tensor("mxb", [128, 2 * T], F32))
        tmpC3 = stP4.enter_context(nc.sbuf_tensor("tmpC3", [128, 2 * D], F32))
        uab = stP4.enter_context(nc.sbuf_tensor("uab", [128, T], F32))
        with stP4:
            ACC = lambda i: mxb[:, (i % 2) * T:(i % 2) * T + T]
            UQ1 = tmpC3[:, 0:D]
            UQ2 = tmpC3[:, D:2 * D]
            SY.wait_ge(s_pe.h, P3_PE)
            SY.wait_ge(s_sp.h, s_sp.n)
            xq0 = s_ld.n
            for b in range(NB):
                s_ld.dma(SY.dma_start(xnT(b), xq_sp[b * 128:(b + 1) * 128, :]))
            XQ_LD = s_ld.n
            SQ.wait_ge(s_pe.h, P3_PE)
            wd0 = s_wld.n
            wdsp = []
            for ch in range(KI):
                buf = wldF[:, (ch % 2) * D:(ch % 2) * D + D]
                qb = wqB[:, (ch % 2) * D:(ch % 2) * D + D]
                if ch >= 2:
                    SQ.wait_ge(s_sp.h, wdsp[ch - 2])
                s_wld.dma(SQ.dma_start(buf, wdT[ch * 128:(ch + 1) * 128, :]))
                V.wait_ge(s_wld.h, wd0 + 16 * (ch + 1))
                if ch == 0:
                    V.wait_ge(s_v.h, V_INVS)
                if ch >= 2:
                    V.wait_ge(s_sp.h, wdsp[ch - 2])
                V.tensor_scalar(UQ1, buf, INVd, 1.0, OP.mult, OP.min)
                V.tensor_scalar(UQ2, UQ1, -1.0, CC, OP.max, OP.add)
                s_v.inc(V.tensor_scalar(qb, UQ2, CC, None, OP.subtract))
                SY.wait_ge(s_v.h, s_v.n)
                s_sp.dma(SY.dma_start(wdq_sp[ch * 128:(ch + 1) * 128, :], qb))
                wdsp.append(s_sp.n)
            WDQ_SP = s_sp.n
            s_row.dma(SY.dma_start(ra3, rows_sp[2:3, :]))
            V.wait_ge(s_row.h, s_row.n)
            V.wait_ge(s_row.h, ROW_SCALES)
            s_v.inc(V.tensor_scalar(rb3, ra3, SCg[0:1, :], None, OP.mult))
            SY.wait_ge(s_v.h, s_v.n)
            s_row.dma(SY.dma_start(rows_sp[3:4, :], rb3))
            SY.wait_ge(s_row.h, s_row.n)
            s_row.dma(SY.dma_start(bc1[:, :], rows_sp[3:4, :].broadcast_to([128, T])))
            ROW_ALPHA = s_row.n
            s_v.inc(V.memset(ACC(1), -1e30))
            V.wait_ge(s_v.h, s_v.n)
            GUBp = lambda i: tmpC3[:, (i % 2) * T:(i % 2) * T + T]
            qU1 = bc0[:, 0:D]
            qU2 = tmpC1[:, T:2 * T]
            WDQ_V = s_v.n
            wg0 = s_wld.n
            W2 = 1024

            def ldgu(ob):
                s_wld.dma(SQ.dma_start(wldF[:, 0:D].rearrange("p (k c) -> p k c", c=128),
                                       castsrc(wgT, ob)))
                s_wld.dma(SQ.dma_start(wldF[:, D:2 * D].rearrange("p (k c) -> p k c", c=128),
                                       castsrc(wuT, ob)))

            def quant3(dst, src, inv):
                V.tensor_scalar(qU1, src, inv, 1.0, OP.mult, OP.min)
                V.tensor_scalar(qU2, qU1, -1.0, CC, OP.max, OP.add)
                s_v.inc(V.tensor_scalar(dst, qU2, CC, None, OP.subtract))
                return s_v.n

            WQG = lambda i: wqB[:, (i % 2) * 2 * D:(i % 2) * 2 * D + D]
            WQU = lambda i: wqB[:, (i % 2) * 2 * D + D:(i % 2) * 2 * D + 2 * D]
            SQ.wait_ge(s_v.h, WDQ_V)
            ldgu(0)
            V.wait_ge(s_wld.h, wg0 + 32)
            V.wait_ge(s_row.h, ROW_ALPHA)
            quant3(WQG(0), wldF[:, 0:D], INVg)
            qmark = [quant3(WQU(0), wldF[:, D:2 * D], INVu)]
            peh = {}
            gub_done = {}
            silu_mark = {}
            gusp = []
            for ob in range(KI):
                if ob + 1 < KI:
                    SQ.wait_ge(s_v.h, qmark[ob])
                    ldgu(ob + 1)
                wqg = WQG(ob).rearrange("p (k c) -> p k c", c=128)
                wqu = WQU(ob).rearrange("p (k c) -> p k c", c=128)
                for h in (0, 1):
                    PE.wait_ge(s_v.h, qmark[ob])
                    if ob == 0 and h == 0:
                        PE.wait_ge(s_ld.h, XQ_LD)
                    if ob >= 1:
                        PE.wait_ge(s_v.h, gub_done[(ob - 1, h)])
                    for tw in (2 * h, 2 * h + 1):
                        for k in range(NB):
                            PE.matmul(BANK(tw), wqg[:, k, :], xnT(k)[:, tw * W:(tw + 1) * W],
                                      start=(k == 0), stop=(k == NB - 1))
                    for tw in (2 * h, 2 * h + 1):
                        for k in range(NB):
                            mm = PE.matmul(BANK(4 + tw), wqu[:, k, :],
                                           xnT(k)[:, tw * W:(tw + 1) * W],
                                           start=(k == 0), stop=(k == NB - 1))
                    s_pe.inc(mm)
                    peh[(ob, h)] = s_pe.n
                for h in (0, 1):
                    V.wait_ge(s_pe.h, peh[(ob, h)])
                    if ob == 0 and h == 0:
                        V.wait_ge(s_row.h, ROW_ALPHA)
                    if ob >= 1:
                        V.wait_ge(s_s.h, silu_mark[(ob - 1, h)])
                    for tw in (2 * h, 2 * h + 1):
                        ins = V.tensor_mul(uab[:, tw * W:(tw + 1) * W], BANK(tw),
                                           bc1[:, tw * W:(tw + 1) * W])
                    s_v.inc(ins)
                    S.wait_ge(s_v.h, s_v.n)
                    if ob + h >= 1:
                        prevh = (ob, 0) if h == 1 else (ob - 1, 1)
                        S.wait_ge(s_v.h, gub_done[prevh])
                    s_s.inc(S.activation(sgb[:, :], uab[:, h * W2:(h + 1) * W2], AF.Silu))
                    silu_mark[(ob, h)] = s_s.n
                    if ob + 1 < KI:
                        V.wait_ge(s_wld.h, wg0 + 32 * (ob + 2))
                        if h == 0:
                            quant3(WQG(ob + 1), wldF[:, 0:D], INVg)
                        else:
                            qmark.append(quant3(WQU(ob + 1), wldF[:, D:2 * D], INVu))
                    V.wait_ge(s_s.h, silu_mark[(ob, h)])
                    if h == 0 and ob >= 2:
                        V.wait_ge(s_sp.h, gusp[ob - 2])
                    for tw in (2 * h, 2 * h + 1):
                        ins = V.tensor_mul(GUBp(ob)[:, tw * W:(tw + 1) * W],
                                           sgb[:, (tw - 2 * h) * W:(tw - 2 * h + 1) * W],
                                           BANK(4 + tw))
                    s_v.inc(ins)
                    gub_done[(ob, h)] = s_v.n
                    gh = GUBp(ob)[:, h * W2:(h + 1) * W2]
                    tmp = bc0[:, h * W2:(h + 1) * W2]
                    V.scalar_tensor_tensor(tmp, gh, -1.0,
                                           ACC(ob + 1)[:, h * W2:(h + 1) * W2],
                                           OP.mult, OP.max)
                    s_v.inc(V.scalar_tensor_tensor(ACC(ob)[:, h * W2:(h + 1) * W2], gh, 1.0,
                                                   tmp, OP.mult, OP.max))
                G.wait_ge(s_v.h, gub_done[(ob, 1)])
                s_sp.dma(G.dma_start(gu_sp[ob * 128:(ob + 1) * 128, :], GUBp(ob)))
                gusp.append(s_sp.n)
            GU_SP = s_sp.n
            top4 = part_max_tree(ACC(KI - 1), ACC(KI), bc0)
            s_v.inc(V.tensor_copy(rb3, top4[0:1, :]))
            s_row.dma(SY.dma_start(R0, rows_sp[2:3, :]))
            V.wait_ge(s_row.h, s_row.n)
            V.tensor_scalar(R1, R0, SCu[0:1, :], None, OP.mult)
            V.tensor_mul(ra3, rb3, R1)
            V.tensor_scalar(ra3, ra3, INV127[0:1, :], EPS, OP.mult, OP.add)
            V.reciprocal(rb3, ra3)
            V.tensor_mul(rb3, R1, rb3)
            s_v.inc(V.tensor_scalar(ra3, ra3, SCd[0:1, :], None, OP.mult))
            SY.wait_ge(s_v.h, s_v.n)
            s_row.dma(SY.dma_start(rows_sp[3:4, :], rb3))
            s_row.dma(SY.dma_start(rows_sp[4:5, :], ra3))
            SY.wait_ge(s_row.h, s_row.n)
            s_row.dma(SY.dma_start(bc0[:, :], rows_sp[3:4, :].broadcast_to([128, T])))
            s_row.dma(SY.dma_start(bc1[:, :], rows_sp[4:5, :].broadcast_to([128, T])))
            ROW_INV2 = s_row.n
            P4_V = s_v.n
        stP5 = ExitStack()
        wdqB = stP5.enter_context(nc.sbuf_tensor("wdqB", [128, 2 * KI * 128], BF16))
        glb = stP5.enter_context(nc.sbuf_tensor("glb", [128, 2 * WQ], F32))
        with stP5:
            q2q = lambda k: actA[:, k * WQ:(k + 1) * WQ]
            SY.wait_ge(s_sp.h, s_sp.n)
            SQ.wait_ge(s_sp.h, s_sp.n)
            SY.wait_ge(s_v.h, P4_V)
            GUL = lambda k: glb[:, (k % 2) * WQ:(k % 2) * WQ + WQ]
            UQ5 = tmpC1[:, 2 * WQ:3 * WQ]
            UD1 = tmpC1[:, 3 * WQ:4 * WQ]
            UD2 = tmpC1[:, 2 * WQ:3 * WQ]
            OAC = lambda db: tmpC1[:, (4 + db % 2) * WQ:(4 + db % 2) * WQ + WQ]
            XRE = lambda db: tmpC1[:, (6 + db % 2) * WQ:(6 + db % 2) * WQ + WQ]
            dn_pe_prev = []
            for tq in range(TQ):
                gq0 = s_ld.n
                vq = []
                for k in range(KI):
                    if k >= 2:
                        SY.wait_ge(s_v.h, vq[k - 2])
                    s_ld.dma(SY.dma_start(GUL(k), gu_sp[k * 128:(k + 1) * 128, tq * WQ:(tq + 1) * WQ]))
                    V.wait_ge(s_ld.h, gq0 + 16 * (k + 1))
                    if k == 0:
                        V.wait_ge(s_row.h, ROW_INV2)
                        if tq > 0:
                            V.wait_ge(s_pe.h, dn_pe_prev[-1])
                        else:
                            V.wait_ge(s_pe.h, s_pe.n)
                    V.tensor_mul(UQ5, GUL(k), bc0[:, tq * WQ:(tq + 1) * WQ])
                    s_v.inc(V.tensor_scalar(q2q(k), UQ5, CC, CC, OP.add, OP.subtract))
                    vq.append(s_v.n)
                Q2_V = s_v.n
                wd5 = s_ld.n
                dn_pe = []
                dn_v = []
                for db in range(NB):
                    wslot = wdqB[:, (db % 2) * KI * 128:((db % 2) + 1) * KI * 128]
                    if db >= 2:
                        SY.wait_ge(s_pe.h, dn_pe[db - 2])
                        SY.wait_ge(s_v.h, dn_v[db - 2])
                    s_ld.dma(SY.dma_start(
                        wslot.rearrange("p (k c) -> p k c", c=128),
                        wdq_sp.rearrange("(k p) (o c) -> p k o c", p=128, c=128)[:, :, db, :]))
                    s_ld.dma(SY.dma_start(OAC(db), oacc_sp[db * 128:(db + 1) * 128,
                                                           tq * WQ:(tq + 1) * WQ]))
                    s_ld.dma(SY.dma_start(XRE(db), xT[db * 128:(db + 1) * 128,
                                                      tq * WQ:(tq + 1) * WQ]))
                    PE.wait_ge(s_ld.h, wd5 + 48 * (db + 1))
                    if db == 0:
                        PE.wait_ge(s_v.h, Q2_V)
                        if tq == 0:
                            PE.wait_ge(s_sp.h, WDQ_SP)
                    if db >= 2:
                        PE.wait_ge(s_v.h, dn_v[db - 2])
                    wv = wslot.rearrange("p (k c) -> p k c", c=128)
                    for k in range(KI):
                        mm = PE.matmul(BANK(db % 2), wv[:, k, :], q2q(k),
                                       start=(k == 0), stop=(k == KI - 1))
                    s_pe.inc(mm)
                    dn_pe.append(s_pe.n)
                    V.wait_ge(s_pe.h, dn_pe[db])
                    V.wait_ge(s_fin.h, s_fin.n)
                    V.tensor_mul(UD1, BANK(db % 2), bc1[:, tq * WQ:(tq + 1) * WQ])
                    V.scalar_tensor_tensor(UD2, UD1, 1.0, OAC(db), OP.mult, OP.add)
                    s_v.inc(V.scalar_tensor_tensor(UD1, UD2, 1.0, XRE(db), OP.mult, OP.add))
                    dn_v.append(s_v.n)
                    SY.wait_ge(s_v.h, dn_v[db])
                    s_fin.dma(SY.dma_start(outT[db * 128:(db + 1) * 128, tq * WQ:(tq + 1) * WQ],
                                           UD1))
                dn_pe_prev = dn_pe
            SY.wait_ge(s_fin.h, s_fin.n)
    return nc


def kernel(L0_attn_x, L0_attn_Wr, L0_attn_Wf, L0_attn_Wc, L0_attn_Wo,
           L0_attn_ln_scale, L0_ffn_w_gate, L0_ffn_w_up, L0_ffn_w_down):
    x = np.ascontiguousarray(np.asarray(L0_attn_x, dtype=np.float32))
    B = x.shape[0]
    shared = dict(
        Wr=np.ascontiguousarray(np.asarray(L0_attn_Wr, dtype=np.float32)),
        Wf=np.ascontiguousarray(np.asarray(L0_attn_Wf, dtype=np.float32)),
        Wc=np.ascontiguousarray(np.asarray(L0_attn_Wc, dtype=np.float32)),
        Wo=np.ascontiguousarray(np.asarray(L0_attn_Wo, dtype=np.float32)),
        lns=np.ascontiguousarray(np.asarray(L0_attn_ln_scale, dtype=np.float32).reshape(16, 128).T),
        wgT=np.ascontiguousarray(np.asarray(L0_ffn_w_gate, dtype=np.float32).T),
        wuT=np.ascontiguousarray(np.asarray(L0_ffn_w_up, dtype=np.float32).T),
        wdT=np.ascontiguousarray(np.asarray(L0_ffn_w_down, dtype=np.float32).T),
    )
    in_maps = [dict(shared, xT=np.ascontiguousarray(x[b].T)) for b in range(B)]
    nc = build()
    res = run_bass_kernel_spmd(nc, in_maps, core_ids=list(range(B)))
    out = np.stack([np.asarray(r["outT"], dtype=np.float32).T for r in res.results], axis=0)
    return np.ascontiguousarray(out)
